# version 02
# baseline (speedup 1.0000x reference)
import math
from contextlib import ExitStack

import numpy as np

import concourse.bass as bass
import concourse.mybir as mybir
from concourse.bass_utils import run_bass_kernel_spmd

F32 = mybir.dt.float32
BF16 = mybir.dt.bfloat16
I8 = mybir.dt.int8
AF = mybir.ActivationFunctionType
OP = mybir.AluOpType

D = 2048
KC = D // 128
H = 16
DH = 128
HID = 8192
JC = HID // 128
SCALE = math.sqrt(DH)
LN_EPS = 1e-5
SRMS_EPS = 1e-12
NPR = 8


class Sem:
    def __init__(self, h):
        self.h = h
        self.v = 0

    def inc(self, n):
        self.v += n
        return self.v


def build_nc(T=1024, taps=()):
    NTC = T // 128
    TT = min(512, T)
    NTT = T // TT
    TH = T // 2
    NTCH = TH // 128

    nc = bass.Bass("TRN2", target_bir_lowering=False)

    x_h = nc.dram_tensor("x", [T, D], F32, kind="ExternalInput")
    wq_h = nc.dram_tensor("wq", [D, D], F32, kind="ExternalInput")
    wk_h = nc.dram_tensor("wk", [D, D], F32, kind="ExternalInput")
    wv_h = nc.dram_tensor("wv", [D, D], F32, kind="ExternalInput")
    wu_h = nc.dram_tensor("wu", [D, D], F32, kind="ExternalInput")
    wo_h = nc.dram_tensor("wo", [D, D], F32, kind="ExternalInput")
    p1_h = nc.dram_tensor("p1", [HID, D], F32, kind="ExternalInput")
    b1_h = nc.dram_tensor("b1", [HID], F32, kind="ExternalInput")
    p2_h = nc.dram_tensor("p2", [D, HID], F32, kind="ExternalInput")
    b2_h = nc.dram_tensor("b2", [D], F32, kind="ExternalInput")
    out_h = nc.dram_tensor("out", [T, D], F32, kind="ExternalOutput")

    x_bf = nc.dram_tensor("x_bf", [T, D], BF16).ap()
    nu_bf = nc.dram_tensor("nu_bf", [T, D], BF16).ap()
    x2_bf = nc.dram_tensor("x2_bf", [T, D], BF16).ap()
    x2f_dr = nc.dram_tensor("x2f_dr", [T, D], F32).ap()
    p1_bf = nc.dram_tensor("p1_bf", [HID, D], BF16).ap()
    p2_bf = nc.dram_tensor("p2_bf", [D, HID], BF16).ap()
    u_dr = nc.dram_tensor("u_dr", [T, D], F32).ap()
    kv_in = nc.dram_tensor("kv_in", [128, D], F32).ap()
    kv_out = nc.dram_tensor("kv_out", [128, D], F32).ap()

    x = x_h.ap()
    out = out_h.ap()
    w_ap = {"q": wq_h.ap(), "k": wk_h.ap(), "v": wv_h.ap(),
            "u": wu_h.ap(), "o": wo_h.ap()}

    R1 = 9216
    R2 = R1 + 65536
    R3 = R2 + 65536
    R4 = R3 + 32768
    R5 = R4 + 8192
    ARENA = R5 + 8192
    nc.alloc_sbuf_tensor("arena", [128, ARENA], I8)
    base = nc.lookup_mloc("arena").addr

    def sbuf(name, shape, dtype, off):
        return nc.alloc_sbuf_tensor_at(name, shape, dtype, offset=base + off)

    off = 0
    kvb = sbuf("kvb", [128, D], BF16, off); off += 4096
    b2b = sbuf("b2b", [1, D], BF16, off); off += 4096
    b1s = sbuf("b1s", [128, JC], F32, off); off += 256
    ones = sbuf("ones", [1, 128], BF16, off); off += 256
    ssq = sbuf("ssq", [128, H], F32, off); off += 64
    st6 = sbuf("st6", [128, 24], F32, off); off += 128
    mv = sbuf("mv", [128, 2], F32, off); off += 64
    rt = sbuf("rt", [128, H], F32, off); off += 64
    s0 = sbuf("s0", [128, H], F32, off); off += 64
    tmpn = sbuf("tmpn", [128, H], F32, off); off += 64
    teps = sbuf("teps", [128, 1], F32, off); off += 64
    assert off <= R1

    xfmb = sbuf("xfmb", [128, KC, T], BF16, R1)
    qfm = sbuf("qfm", [128, KC, T], BF16, R1 + 32768)
    hfmb = sbuf("hfmb", [128, JC, TH], BF16, R1)
    ktm = sbuf("ktm", [128, NTC, D], BF16, R2)
    kvf = sbuf("kvf", [128, D], F32, R2)
    vtm = sbuf("vtm", [128, NTC, D], BF16, R2 + NTC * D * 2)
    x2h = sbuf("x2h", [128, NTCH, D], F32, R2)
    nuT = sbuf("nuT", [128, KC, T], BF16, R2 + 32768)
    p1T = sbuf("p1T", [128, 2, 16, 512], BF16, R2 + 32768)
    p2Tr = sbuf("p2Tr", [128, NPR, 512], BF16, R3 + 16384)
    outt = sbuf("outt", [128, D], F32, R3 + 24576)
    wstr = sbuf("wstr", [128, 2, 16, 512], BF16, R3)
    urld = sbuf("urld", [128, 2, D], F32, R3)
    x2fmb = sbuf("x2fmb", [128, KC, TH], BF16, R3)
    ubnc = sbuf("ubnc", [128, 4, 512], F32, R4)
    xres = sbuf("xres", [128, D], F32, R4)
    nut = sbuf("nut", [128, 2, D], BF16, R5)

    prog = {k: [] for k in ("tensor", "scalar", "vector", "sync", "gpsimd")}

    def em(eng, fn):
        prog[eng].append(fn)

    def wait(eng, sem, val):
        if val > 0:
            em(eng, lambda e, s=sem.h, v=val: e.wait_ge(s, v))

    def winc(eng, sem, n=1):
        t = sem.inc(n)
        last = prog[eng].pop()
        em(eng, lambda e, f=last, s=sem.h, m=n: f(e).then_inc(s, m))
        return t

    with ExitStack() as stack:
        def mksem(name):
            return Sem(stack.enter_context(nc.semaphore(name)))

        sTe, sSc, sVe = mksem("sTe"), mksem("sSc"), mksem("sVe")
        sCc = mksem("sCc")
        ENG_SEM = {"sc": sSc, "ve": sVe, "te": sTe}
        sXbf = mksem("sXbf")
        sCst = mksem("sCst")
        sFF = mksem("sFF")
        sKv1, sKv2 = mksem("sKv1"), mksem("sKv2")
        sW = [mksem("sW0"), mksem("sW1")]
        sUb = [mksem(f"sUb{i}") for i in range(4)]
        sUr = [mksem("sUr0"), mksem("sUr1")]
        sNu = mksem("sNu")
        sX2h = [mksem("sX2a"), mksem("sX2b")]
        sXf2 = [mksem("sXfa"), mksem("sXfb")]
        sXrl = [mksem("sXra"), mksem("sXrb")]
        sXr = mksem("sXr")
        sSt = mksem("sSt")
        sTx = mksem("sTx")
        sTn = mksem("sTn")
        sTf = mksem("sTf")
        sTp = [mksem("sTp0"), mksem("sTp1")]
        sP = [mksem(f"sP{i}") for i in range(NPR)]

        def gdma(sem, dst, src, **kw):
            t = sem.inc(16)
            em("gpsimd", lambda e, d=dst, s=src, k=kw:
               e.dma_start(out=d, in_=s, **k).then_inc(sem.h, 16))
            return t

        def ytr(sem, dst, src):
            t = sem.inc(16)
            em("sync", lambda e, d=dst, s=src:
               e.dma_start_transpose(d, s).then_inc(sem.h, 16))
            return t

        def str_(sem, dst, src):
            t = sem.inc(16)
            em("scalar", lambda e, d=dst, s=src:
               e.dma_start_transpose(d, s).then_inc(sem.h, 16))
            return t

        def sdma(sem, dst, src, **kw):
            t = sem.inc(16)
            em("scalar", lambda e, d=dst, s=src, k=kw:
               e.dma_start(out=d, in_=s, **k).then_inc(sem.h, 16))
            return t

        gdma(sXbf, x_bf, x)
        for j in range(JC):
            gdma(sCst, b1s[:, j:j + 1], b1_h.ap()[j * 128:(j + 1) * 128, None])
        gdma(sCst, b2b[0:1, :], b2_h.ap()[None, :])
        em("vector", lambda e: e.memset(ones[:], 1.0))
        t_ones = winc("vector", sVe)

        wait("sync", sXbf, sXbf.v)
        for kc in range(KC):
            ytr(sTx, xfmb[:, kc, :], x_bf[:, kc * 128:(kc + 1) * 128])

        ffn_cast = []
        for i in range(8):
            ffn_cast.append((p1_bf[i * 1024:(i + 1) * 1024, :],
                             p1_h.ap()[i * 1024:(i + 1) * 1024, :]))
        for i in range(8):
            ffn_cast.append((p2_bf[i * 256:(i + 1) * 256, :],
                             p2_h.ap()[i * 256:(i + 1) * 256, :]))

        def pop_casts(n):
            for _ in range(n):
                d, s = ffn_cast.pop(0)
                gdma(sFF, d, s)

        with (
            nc.psum_tensor("kvps", [128, D], F32) as kvps,
            nc.psum_tensor("pj0", [128, 512], F32) as pj0,
            nc.psum_tensor("pj1", [128, 512], F32) as pj1,
        ):
            pj = [pj0, pj1]

            state = {"oscnt": 0}
            slot_release = [None, None]
            load_done = {}

            def load_wtiles(proj, wap, os, slot):
                if slot_release[slot] is not None:
                    wait("gpsimd", sTe, slot_release[slot])
                for k in range(16):
                    gdma(sW[slot], wstr[:, slot, k, :],
                         wap[k * 128:(k + 1) * 128, os * 512:(os + 1) * 512])
                load_done[(proj, os)] = (slot, sW[slot].v)

            def wait_wtiles(proj, os):
                slot, tot = load_done[(proj, os)]
                wait("tensor", sW[slot], tot)
                return slot

            pjcnt = [0]
            pj_release = {}

            def pj_guard(idx):
                if idx >= 2 and pj_release.get(idx - 2) is not None:
                    kind, tgt = pj_release[idx - 2]
                    wait("tensor", ENG_SEM[kind], tgt)

            def stream_proj_tm(proj, wap, evac):
                for os in range(4):
                    slot = state["oscnt"] % 2
                    load_wtiles(proj, wap, os, slot)
                    wait_wtiles(proj, os)
                    for tc in range(NTC):
                        idx = pjcnt[0]
                        ps = pj[idx % 2]
                        pj_guard(idx)
                        for k in range(16):
                            em("tensor", lambda e, p=ps,
                               l=xfmb[:, k, tc * 128:(tc + 1) * 128],
                               r=wstr[:, slot, k, :], st=(k == 0), sp=(k == 15):
                               e.matmul(p[:], l, r, start=st, stop=sp))
                        t = winc("tensor", sTe)
                        if tc == NTC - 1:
                            slot_release[slot] = t
                        pj_release[idx] = evac(os, tc, ps, t)
                        pjcnt[0] += 1
                    state["oscnt"] += 1

            def evac_k(os, tc, ps, t):
                wait("scalar", sTe, t)
                em("scalar", lambda e, o=ktm[:, tc, os * 512:(os + 1) * 512], p=ps:
                   e.activation(o, p[:], AF.Relu, scale=1.0 / SCALE))
                return ("sc", winc("scalar", sSc))

            def evac_v(os, tc, ps, t):
                wait("vector", sTe, t)
                em("vector", lambda e, o=vtm[:, tc, os * 512:(os + 1) * 512], p=ps:
                   e.tensor_copy(o, p[:]))
                return ("ve", winc("vector", sVe))

            ucnt = [0]

            def evac_u(os, tc, ps, t):
                ui = ucnt[0] % 4
                wait("scalar", sTe, t)
                if ucnt[0] >= 4:
                    wait("scalar", sUb[ui], sUb[ui].v)
                em("scalar", lambda e, o=ubnc[:, ui, :], p=ps:
                   e.activation(o, p[:], AF.Silu))
                tsc = winc("scalar", sSc)
                wait("gpsimd", sSc, tsc)
                gdma(sUb[ui], u_dr[tc * 128:(tc + 1) * 128, os * 512:(os + 1) * 512],
                     ubnc[:, ui, :])
                ucnt[0] += 1
                return ("sc", tsc)

            wait("tensor", sTx, sTx.v)

            stream_proj_tm("k", w_ap["k"], evac_k)
            pop_casts(4)
            stream_proj_tm("v", w_ap["v"], evac_v)
            pop_casts(4)

            wait("tensor", sSc, sSc.v)
            wait("tensor", sVe, sVe.v)
            for h in range(H):
                for tcc in range(NTC):
                    em("tensor", lambda e, o=kvps[:, h * 128:(h + 1) * 128],
                       l=ktm[:, tcc, h * 128:(h + 1) * 128],
                       r=vtm[:, tcc, h * 128:(h + 1) * 128],
                       st=(tcc == 0), sp=(tcc == NTC - 1):
                       e.matmul(o, l, r, start=st, stop=sp))
            t_kvmm = winc("tensor", sTe)
            wait("vector", sTe, t_kvmm)
            em("vector", lambda e: e.tensor_copy(kvf[:], kvps[:]))
            t_kvf = winc("vector", sVe)
            wait("gpsimd", sVe, t_kvf)
            gdma(sKv1, kv_in, kvf[:])
            wait("gpsimd", sKv1, sKv1.v)
            em("gpsimd", lambda e: e.collective_compute(
                "AllReduce", OP.add,
                replica_groups=[[0, 1, 2, 3], [4, 5, 6, 7]],
                ins=[kv_in], outs=[kv_out]))
            t_cc = winc("gpsimd", sCc)
            wait("gpsimd", sCc, t_cc)
            gdma(sKv2, kvb[:], kv_out)

            stream_proj_tm("u", w_ap["u"], evac_u)
            pop_casts(4)

            for os in range(4):
                slot = state["oscnt"] % 2
                load_wtiles("q", w_ap["q"], os, slot)
                wait_wtiles("q", os)
                for oc in range(4):
                    for tt in range(NTT):
                        idx = pjcnt[0]
                        ps = pj[idx % 2]
                        pj_guard(idx)
                        for k in range(16):
                            em("tensor", lambda e, p=ps,
                               l=wstr[:, slot, k, oc * 128:(oc + 1) * 128],
                               r=xfmb[:, k, tt * TT:(tt + 1) * TT],
                               st=(k == 0), sp=(k == 15):
                               e.matmul(p[:, :TT], l, r, start=st, stop=sp))
                        t = winc("tensor", sTe)
                        if oc == 3 and tt == NTT - 1:
                            slot_release[slot] = t
                        ocg = os * 4 + oc
                        wait("scalar", sTe, t)
                        em("scalar", lambda e,
                           o=qfm[:, ocg, tt * TT:(tt + 1) * TT], p=ps:
                           e.activation(o, p[:, :TT], AF.Relu, scale=1.0 / SCALE))
                        pj_release[idx] = ("sc", winc("scalar", sSc))
                        pjcnt[0] += 1
                state["oscnt"] += 1
            pop_casts(4)
            assert not ffn_cast
            for back in (2, 1):
                r = pj_release.get(pjcnt[0] - back)
                if r is not None:
                    kind, tgt = r
                    wait("tensor", ENG_SEM[kind], tgt)

        t_q_evacs = sSc.v

        with (
            nc.psum_tensor("aops0", [128, D], F32) as aops0,
            nc.psum_tensor("aops1", [128, D], F32) as aops1,
        ):
            aops = [aops0, aops1]
            wait("tensor", sVe, sVe.v)
            wait("gpsimd", sTe, sTe.v)
            for i in range(4):
                wait("gpsimd", sUb[i], sUb[i].v)
            wait("tensor", sKv2, sKv2.v)
            wait("tensor", sSc, t_q_evacs)
            nu_written = []
            aops_release = [None, None]
            urld_release = [None, None]
            prev_tnu = 0
            for tc in range(NTC):
                sl = tc % 2
                if urld_release[sl] is not None:
                    wait("gpsimd", sVe, urld_release[sl])
                gdma(sUr[sl], urld[:, sl, :], u_dr[tc * 128:(tc + 1) * 128, :])
                t_u = sUr[sl].v
                if aops_release[sl] is not None:
                    wait("tensor", sVe, aops_release[sl])
                for h in range(H):
                    em("tensor", lambda e, o=aops[sl][:, h * 128:(h + 1) * 128],
                       l=qfm[:, h, tc * 128:(tc + 1) * 128],
                       r=kvb[:, h * 128:(h + 1) * 128]:
                       e.matmul(o, l, r, start=True, stop=True))
                t_ao = winc("tensor", sTe)
                wait("scalar", sTe, t_ao)
                if prev_tnu:
                    wait("scalar", sVe, prev_tnu)
                if tc >= 2:
                    wait("scalar", sNu, sNu.v)
                for h in range(H):
                    em("scalar", lambda e, o=nut[:, sl, h * 128:(h + 1) * 128],
                       p=aops[sl][:, h * 128:(h + 1) * 128], a=ssq[:, h:h + 1]:
                       e.activation(o, p, AF.Square, accum_out=a))
                t_sq = winc("scalar", sSc)
                wait("scalar", sSc, t_sq)
                em("scalar", lambda e: e.activation(rt[:], ssq[:], AF.Sqrt,
                                                    scale=1.0 / DH))
                t_rt = winc("scalar", sSc)
                wait("vector", sSc, t_rt)
                em("vector", lambda e: e.tensor_scalar_max(rt[:], rt[:], SRMS_EPS))
                t1 = winc("vector", sVe)
                wait("vector", sVe, t1)
                em("vector", lambda e: e.reciprocal(s0[:], rt[:]))
                t2 = winc("vector", sVe)
                wait("vector", sVe, t2)
                em("vector", lambda e: e.tensor_mul(tmpn[:], s0[:], s0[:]))
                t3 = winc("vector", sVe)
                wait("vector", sVe, t3)
                em("vector", lambda e: e.tensor_mul(tmpn[:], tmpn[:], ssq[:]))
                t4 = winc("vector", sVe)
                wait("vector", sVe, t4)
                em("vector", lambda e: e.tensor_scalar(tmpn[:], tmpn[:],
                                                       -0.5 / DH, 1.5, OP.mult, OP.add))
                t5 = winc("vector", sVe)
                wait("vector", sVe, t5)
                em("vector", lambda e: e.tensor_mul(s0[:], s0[:], tmpn[:]))
                t6 = winc("vector", sVe)
                wait("vector", sVe, t6)
                wait("vector", sUr[sl], t_u)
                for h in range(H):
                    em("vector", lambda e, o=nut[:, sl, h * 128:(h + 1) * 128],
                       p=aops[sl][:, h * 128:(h + 1) * 128], s=s0[:, h:h + 1],
                       u=urld[:, sl, h * 128:(h + 1) * 128]:
                       e.scalar_tensor_tensor(o, p, s, u, OP.mult, OP.mult))
                t_nu = winc("vector", sVe)
                prev_tnu = t_nu
                aops_release[sl] = t_nu
                urld_release[sl] = t_nu
                wait("gpsimd", sVe, t_nu)
                nu_written.append(gdma(sNu, nu_bf[tc * 128:(tc + 1) * 128, :],
                                       nut[:, sl, :]))

        wait("sync", sNu, sNu.v)
        for kc in range(KC):
            ytr(sTn, nuT[:, kc, :], nu_bf[:, kc * 128:(kc + 1) * 128])

        with (
            nc.psum_tensor("yps0", [128, 512], F32) as yps0,
            nc.psum_tensor("yps1", [128, 512], F32) as yps1,
        ):
            yps = [yps0, yps1]
            ycnt = 0
            y_release = {}
            xres_release = None
            x2h_free = None
            wait("tensor", sTn, sTn.v)
            wait("tensor", sVe, sVe.v)
            wait("tensor", sSc, sSc.v)
            wait("gpsimd", sVe, sVe.v)
            for hf in range(2):
                tc0 = hf * NTCH
                for os in range(4):
                    slot = state["oscnt"] % 2
                    load_wtiles(("o", hf), w_ap["o"], os, slot)
                    wait_wtiles(("o", hf), os)
                    for tc4 in range(NTCH):
                        tc = tc0 + tc4
                        ps = yps[ycnt % 2]
                        if ycnt >= 2 and y_release.get(ycnt - 2) is not None:
                            wait("tensor", sSc, y_release[ycnt - 2])
                        for k in range(16):
                            em("tensor", lambda e, p=ps,
                               l=nuT[:, k, tc * 128:(tc + 1) * 128],
                               r=wstr[:, slot, k, :], st=(k == 0), sp=(k == 15):
                               e.matmul(p[:], l, r, start=st, stop=sp))
                        t = winc("tensor", sTe)
                        if tc4 == NTCH - 1:
                            slot_release[slot] = t
                        wait("scalar", sTe, t)
                        if hf == 1 and os == 0 and x2h_free is not None:
                            wait("scalar", sX2h[0], sX2h[0].v)
                            wait("scalar", sXf2[0], sXf2[0].v)
                        em("scalar", lambda e,
                           o=x2h[:, tc4, os * 512:(os + 1) * 512],
                           p=ps: e.copy(o, p[:]))
                        y_release[ycnt] = winc("scalar", sSc)
                        ycnt += 1
                    state["oscnt"] += 1
                t_evac_half = sSc.v
                wait("vector", sSc, t_evac_half)
                for tc4 in range(NTCH):
                    tc = tc0 + tc4
                    if xres_release is not None:
                        wait("gpsimd", sVe, xres_release)
                    gdma(sXr, xres[:], x[tc * 128:(tc + 1) * 128, :])
                    wait("vector", sXr, sXr.v)
                    em("vector", lambda e, o=x2h[:, tc4, :]:
                       e.tensor_add(o, o, xres[:]))
                    t_add = winc("vector", sVe)
                    xres_release = t_add
                    wait("vector", sVe, t_add)
                    for i in range(4):
                        em("vector", lambda e, o=st6[:, 6 * i:6 * (i + 1)],
                           s=x2h[:, tc4, 512 * i:512 * (i + 1)]:
                           e.bn_stats(o, s))
                    t_st = winc("vector", sVe)
                    wait("vector", sVe, t_st)
                    em("vector", lambda e: e.bn_aggr(mv[:], st6[:]))
                    t_mv = winc("vector", sVe)
                    wait("vector", sVe, t_mv)
                    em("vector", lambda e: e.tensor_scalar_add(teps[:], mv[:, 1:2],
                                                               LN_EPS))
                    t2 = winc("vector", sVe)
                    wait("scalar", sVe, t2)
                    em("scalar", lambda e: e.activation(rt[:, 0:1], teps[:], AF.Sqrt))
                    t_rt2 = winc("scalar", sSc)
                    wait("vector", sSc, t_rt2)
                    em("vector", lambda e: e.reciprocal(s0[:, 0:1], rt[:, 0:1]))
                    t1 = winc("vector", sVe)
                    wait("vector", sVe, t1)
                    em("vector", lambda e: e.tensor_mul(tmpn[:, 0:1], s0[:, 0:1],
                                                        s0[:, 0:1]))
                    t3 = winc("vector", sVe)
                    wait("vector", sVe, t3)
                    em("vector", lambda e: e.tensor_mul(tmpn[:, 0:1], tmpn[:, 0:1],
                                                        teps[:]))
                    t4 = winc("vector", sVe)
                    wait("vector", sVe, t4)
                    em("vector", lambda e: e.tensor_scalar(tmpn[:, 0:1], tmpn[:, 0:1],
                                                           -0.5, 1.5, OP.mult, OP.add))
                    t5 = winc("vector", sVe)
                    wait("vector", sVe, t5)
                    em("vector", lambda e: e.tensor_mul(s0[:, 0:1], s0[:, 0:1],
                                                        tmpn[:, 0:1]))
                    t6 = winc("vector", sVe)
                    wait("vector", sVe, t6)
                    em("vector", lambda e, o=x2h[:, tc4, :]:
                       e.tensor_scalar(o, o, mv[:, 0:1], s0[:, 0:1],
                                       OP.subtract, OP.mult))
                    t_ln = winc("vector", sVe)
                    wait("gpsimd", sVe, t_ln)
                    gdma(sX2h[hf], x2_bf[tc * 128:(tc + 1) * 128, :], x2h[:, tc4, :])
                    gdma(sXf2[hf], x2f_dr[tc * 128:(tc + 1) * 128, :], x2h[:, tc4, :])
                x2h_free = True

        with nc.psum_tensor("fps", [128, 8, 512], F32) as fps:
            p1T_release = [None, None]
            p2Tr_release = [None] * NPR
            bank_rel = {}
            out_stored = 0
            wait("sync", sTe, sTe.v)
            wait("sync", sFF, sFF.v)
            wait("scalar", sFF, sFF.v)
            wait("tensor", sSc, sSc.v)
            x2fmb_release = [None]
            ln2_done = 0
            for hf in range(2):
                tc0 = hf * NTCH
                wait("sync", sX2h[hf], sX2h[hf].v)
                if hf == 1:
                    wait("sync", sTe, x2fmb_release[0])
                    wait("sync", sVe, sVe.v)
                    wait("sync", sSc, sSc.v)
                    wait("sync", sTe, sTe.v)
                for kc in range(KC):
                    ytr(sTf, x2fmb[:, kc, :], x2_bf[hf * TH:(hf + 1) * TH,
                                                    kc * 128:(kc + 1) * 128])
                t_x2f = sTf.v
                x2fmb_release = [None]
                h_done = {}
                wait("tensor", sTf, t_x2f)
                wait("scalar", sCst, sCst.v)
                for jg in range(16):
                    sl = jg % 2
                    if p1T_release[sl] is not None:
                        wait("sync", sTe, p1T_release[sl])
                    for e_ in range(16):
                        ytr(sTp[sl], p1T[:, sl, e_, :],
                            p1_bf[jg * 512:(jg + 1) * 512, e_ * 128:(e_ + 1) * 128])
                    wait("tensor", sTp[sl], sTp[sl].v)
                    for j4 in range(4):
                        jc = jg * 4 + j4
                        bank = jc % 8
                        if bank_rel.get(bank) is not None:
                            kind, tgt = bank_rel[bank]
                            wait("tensor", ENG_SEM[kind], tgt)
                        for e_ in range(16):
                            em("tensor", lambda e, o=fps[:, bank, :TH],
                               l=p1T[:, sl, e_, j4 * 128:(j4 + 1) * 128],
                               r=x2fmb[:, e_, :], st=(e_ == 0), sp=(e_ == 15):
                               e.matmul(o, l, r, start=st, stop=sp))
                        t = winc("tensor", sTe)
                        if j4 == 3:
                            p1T_release[sl] = t
                        if jg == 15 and j4 == 3:
                            x2fmb_release[0] = t
                        wait("scalar", sTe, t)
                        em("scalar", lambda e, o=hfmb[:, jc, :], p=fps[:, bank, :TH],
                           b=b1s[:, jc:jc + 1]:
                           e.activation(o, p, AF.Gelu, bias=b))
                        h_done[jc] = winc("scalar", sSc)
                        bank_rel[bank] = ("sc", h_done[jc])
                for s_ in (sXf2[0], sXf2[1], sX2h[0], sX2h[1]):
                    wait("gpsimd", s_, s_.v)
                if hf == 1:
                    wait("gpsimd", sVe, ln2_done)
                for tc4 in range(NTCH):
                    tc = tc0 + tc4
                    gdma(sXrl[hf], x2h[:, tc4, :], x2f_dr[tc * 128:(tc + 1) * 128, :])
                wait("tensor", sCst, sCst.v)
                wait("vector", sVe, t_ones)
                wait("vector", sXrl[hf], sXrl[hf].v)
                es_evac = {}
                for es in range(4):
                    for tc4 in range(NTCH):
                        bank = tc4
                        if bank_rel.get(bank) is not None:
                            kind, tgt = bank_rel[bank]
                            wait("tensor", ENG_SEM[kind], tgt)
                        em("tensor", lambda e, o=fps[:, bank, :],
                           r=b2b[0:1, es * 512:(es + 1) * 512]:
                           e.matmul(o, ones[0:1, :], r, start=True, stop=False))
                    for j in range(JC):
                        rsl = j % NPR
                        if es == 0:
                            wait("tensor", sSc, h_done[j])
                        if p2Tr_release[rsl] is not None:
                            wait("scalar", sTe, p2Tr_release[rsl])
                        str_(sP[rsl], p2Tr[:, rsl, :],
                             p2_bf[es * 512:(es + 1) * 512, j * 128:(j + 1) * 128])
                        wait("tensor", sP[rsl], sP[rsl].v)
                        for tc4 in range(NTCH):
                            em("tensor", lambda e, o=fps[:, tc4, :],
                               l=hfmb[:, j, tc4 * 128:(tc4 + 1) * 128],
                               r=p2Tr[:, rsl, :], sp=(j == JC - 1):
                               e.matmul(o, l, r, start=False, stop=sp))
                        p2Tr_release[rsl] = winc("tensor", sTe)
                    t_es = sTe.v
                    for tc4 in range(NTCH):
                        wait("vector", sTe, t_es)
                        em("vector", lambda e,
                           o=x2h[:, tc4, es * 512:(es + 1) * 512],
                           p=fps[:, tc4, :]: e.tensor_add(o, o, p))
                        es_evac[(es, tc4)] = winc("vector", sVe)
                        bank_rel[tc4] = ("ve", es_evac[(es, tc4)])
                for tc4 in range(NTCH):
                    tc = tc0 + tc4
                    wait("vector", sVe, es_evac[(3, tc4)])
                    for i in range(4):
                        em("vector", lambda e, o=st6[:, 6 * i:6 * (i + 1)],
                           s=x2h[:, tc4, 512 * i:512 * (i + 1)]:
                           e.bn_stats(o, s))
                    t_st = winc("vector", sVe)
                    wait("vector", sVe, t_st)
                    em("vector", lambda e: e.bn_aggr(mv[:], st6[:]))
                    t_mv = winc("vector", sVe)
                    wait("vector", sVe, t_mv)
                    em("vector", lambda e: e.tensor_scalar_add(teps[:], mv[:, 1:2],
                                                               LN_EPS))
                    t2 = winc("vector", sVe)
                    wait("scalar", sVe, t2)
                    em("scalar", lambda e: e.activation(rt[:, 0:1], teps[:], AF.Sqrt))
                    t_rt2 = winc("scalar", sSc)
                    wait("vector", sSc, t_rt2)
                    em("vector", lambda e: e.reciprocal(s0[:, 0:1], rt[:, 0:1]))
                    t1 = winc("vector", sVe)
                    wait("vector", sVe, t1)
                    em("vector", lambda e: e.tensor_mul(tmpn[:, 0:1], s0[:, 0:1],
                                                        s0[:, 0:1]))
                    t3 = winc("vector", sVe)
                    wait("vector", sVe, t3)
                    em("vector", lambda e: e.tensor_mul(tmpn[:, 0:1], tmpn[:, 0:1],
                                                        teps[:]))
                    t4 = winc("vector", sVe)
                    wait("vector", sVe, t4)
                    em("vector", lambda e: e.tensor_scalar(tmpn[:, 0:1], tmpn[:, 0:1],
                                                           -0.5, 1.5, OP.mult, OP.add))
                    t5 = winc("vector", sVe)
                    wait("vector", sVe, t5)
                    em("vector", lambda e: e.tensor_mul(s0[:, 0:1], s0[:, 0:1],
                                                        tmpn[:, 0:1]))
                    t6 = winc("vector", sVe)
                    wait("vector", sVe, t6)
                    if out_stored:
                        wait("vector", sSt, sSt.v)
                    em("vector", lambda e, s=x2h[:, tc4, :]:
                       e.tensor_scalar(outt[:], s, mv[:, 0:1], s0[:, 0:1],
                                       OP.subtract, OP.mult))
                    t_o = winc("vector", sVe)
                    ln2_done = t_o
                    wait("scalar", sVe, t_o)
                    sdma(sSt, out[tc * 128:(tc + 1) * 128, :], outt[:])
                    out_stored += 1

        if taps:
            tapable = {"x_bf": (x_bf, [T, D], BF16), "u_dr": (u_dr, [T, D], F32),
                       "kv_out": (kv_out, [128, D], F32), "nu_bf": (nu_bf, [T, D], BF16),
                       "x2_bf": (x2_bf, [T, D], BF16), "x2f_dr": (x2f_dr, [T, D], F32),
                       "p1_bf": (p1_bf, [HID, D], BF16), "p2_bf": (p2_bf, [D, HID], BF16),
                       "sb_hfmb": (hfmb[:, :, :], [128, JC * TH], BF16),
                       "sb_x2fmb": (x2fmb[:, :, :], [128, KC * TH], BF16),
                       "sb_x2h": (x2h[:, :, :], [128, NTCH * D], F32),
                       "sb_p1T": (p1T[:, :, :, :], [128, 2 * 16 * 512], BF16)}
            sTap = mksem("sTap")
            for s in ([sXbf, sCst, sFF, sKv1, sKv2, sNu, sXr, sSt, sTx, sTn, sTf]
                      + sW + sUb + sUr + sX2h + sXf2 + sXrl + sTp + sP):
                wait("sync", s, s.v)
            for name in taps:
                ap, shp, dt = tapable[name]
                if name.startswith("sb_"):
                    shp = [128, shp[1]] if len(shp) == 2 else shp
                th = nc.dram_tensor("tap_" + name, shp, dt, kind="ExternalOutput")
                t = sTap.inc(16)
                em("sync", lambda e, d=th.ap(), s=ap: e.dma_start(out=d, in_=s).then_inc(sTap.h, 16))
            wait("sync", sTap, sTap.v)

        wait("sync", sSt, sSt.v)
        for s in ([sXbf, sCst, sFF, sKv1, sKv2, sNu, sXr, sTx, sTn, sTf]
                  + sW + sUb + sUr + sX2h + sXf2 + sXrl + sTp + sP):
            wait("sync", s, s.v)

        with nc.Block() as block:
            @block.tensor
            def _(e):
                for f in prog["tensor"]:
                    f(e)

            @block.scalar
            def _(e):
                for f in prog["scalar"]:
                    f(e)

            @block.vector
            def _(e):
                for f in prog["vector"]:
                    f(e)

            @block.sync
            def _(e):
                for f in prog["sync"]:
                    f(e)

            @block.gpsimd
            def _(e):
                for f in prog["gpsimd"]:
                    f(e)

    return nc


_NC_CACHE = {}


def kernel(L0_attn_x, L0_attn_Wq, L0_attn_Wk, L0_attn_Wv, L0_attn_Wu,
           L0_attn_Wo, L0_ffn_proj1, L0_ffn_proj1_bias, L0_ffn_proj2,
           L0_ffn_proj2_bias, L0_ffn_gate_w):
    T = 1024
    x = np.ascontiguousarray(np.asarray(L0_attn_x, dtype=np.float32))
    B, L, _ = x.shape
    per = {
        "wq": np.ascontiguousarray(np.asarray(L0_attn_Wq, dtype=np.float32)),
        "wk": np.ascontiguousarray(np.asarray(L0_attn_Wk, dtype=np.float32)),
        "wv": np.ascontiguousarray(np.asarray(L0_attn_Wv, dtype=np.float32)),
        "wu": np.ascontiguousarray(np.asarray(L0_attn_Wu, dtype=np.float32)),
        "wo": np.ascontiguousarray(np.asarray(L0_attn_Wo, dtype=np.float32)),
        "p1": np.ascontiguousarray(np.asarray(L0_ffn_proj1, dtype=np.float32)),
        "b1": np.ascontiguousarray(np.asarray(L0_ffn_proj1_bias, dtype=np.float32)),
        "p2": np.ascontiguousarray(np.asarray(L0_ffn_proj2, dtype=np.float32)),
        "b2": np.ascontiguousarray(np.asarray(L0_ffn_proj2_bias, dtype=np.float32)),
    }
    if T not in _NC_CACHE:
        _NC_CACHE[T] = build_nc(T)
    nc = _NC_CACHE[T]
    in_maps = []
    for c in range(8):
        b = c // 4
        t0 = (c % 4) * T
        m = dict(per)
        m["x"] = np.ascontiguousarray(x[b, t0:t0 + T])
        in_maps.append(m)
    res = run_bass_kernel_spmd(nc, in_maps, core_ids=list(range(8)))
    full = np.empty((B, L, D), dtype=np.float32)
    for c in range(8):
        b = c // 4
        t0 = (c % 4) * T
        full[b, t0:t0 + T] = res.results[c]["out"]
    return full


# version 04
# speedup vs baseline: 1.0336x; 1.0336x over previous
import math
from contextlib import ExitStack

import numpy as np

import concourse.bass as bass
import concourse.mybir as mybir
from concourse.bass_utils import run_bass_kernel_spmd

F32 = mybir.dt.float32
BF16 = mybir.dt.bfloat16
I8 = mybir.dt.int8
AF = mybir.ActivationFunctionType
OP = mybir.AluOpType

D = 2048
KC = D // 128
H = 16
DH = 128
HID = 8192
JC = HID // 128
SCALE = math.sqrt(DH)
LN_EPS = 1e-5
SRMS_EPS = 1e-12
NPR = 8


class Sem:
    def __init__(self, h):
        self.h = h
        self.v = 0

    def inc(self, n):
        self.v += n
        return self.v


def build_nc(T=1024, taps=()):
    NTC = T // 128
    TT = min(512, T)
    NTT = T // TT
    TH = T // 2
    NTCH = TH // 128

    nc = bass.Bass("TRN2", target_bir_lowering=False)

    x_h = nc.dram_tensor("x", [T, D], F32, kind="ExternalInput")
    wq_h = nc.dram_tensor("wq", [D, D], F32, kind="ExternalInput")
    wk_h = nc.dram_tensor("wk", [D, D], F32, kind="ExternalInput")
    wv_h = nc.dram_tensor("wv", [D, D], F32, kind="ExternalInput")
    wu_h = nc.dram_tensor("wu", [D, D], F32, kind="ExternalInput")
    wo_h = nc.dram_tensor("wo", [D, D], F32, kind="ExternalInput")
    p1_h = nc.dram_tensor("p1", [HID, D], F32, kind="ExternalInput")
    b1_h = nc.dram_tensor("b1", [HID], F32, kind="ExternalInput")
    p2_h = nc.dram_tensor("p2", [D, HID], F32, kind="ExternalInput")
    b2_h = nc.dram_tensor("b2", [D], F32, kind="ExternalInput")
    out_h = nc.dram_tensor("out", [T, D], F32, kind="ExternalOutput")

    x_bf = nc.dram_tensor("x_bf", [T, D], BF16).ap()
    nu_bf = nc.dram_tensor("nu_bf", [T, D], BF16).ap()
    x2_bf = nc.dram_tensor("x2_bf", [T, D], BF16).ap()
    x2f_dr = nc.dram_tensor("x2f_dr", [T, D], F32).ap()
    p1_bf = nc.dram_tensor("p1_bf", [HID, D], BF16).ap()
    p2_bf = nc.dram_tensor("p2_bf", [D, HID], BF16).ap()
    u_dr = nc.dram_tensor("u_dr", [T, D], F32).ap()
    kv_in = nc.dram_tensor("kv_in", [128, D], F32).ap()
    kv_out = nc.dram_tensor("kv_out", [128, D], F32).ap()

    x = x_h.ap()
    out = out_h.ap()
    w_ap = {"q": wq_h.ap(), "k": wk_h.ap(), "v": wv_h.ap(),
            "u": wu_h.ap(), "o": wo_h.ap()}

    R1 = 9216
    R2 = R1 + 65536
    R3 = R2 + 65536
    R4 = R3 + 32768
    R5 = R4 + 8192
    ARENA = R5 + 8192
    nc.alloc_sbuf_tensor("arena", [128, ARENA], I8)
    base = nc.lookup_mloc("arena").addr

    def sbuf(name, shape, dtype, off):
        return nc.alloc_sbuf_tensor_at(name, shape, dtype, offset=base + off)

    off = 0
    kvb = sbuf("kvb", [128, D], BF16, off); off += 4096
    b2b = sbuf("b2b", [1, D], BF16, off); off += 4096
    b1s = sbuf("b1s", [128, JC], F32, off); off += 256
    ones = sbuf("ones", [1, 128], BF16, off); off += 256
    ssq = sbuf("ssq", [128, H], F32, off); off += 64
    st6 = sbuf("st6", [128, 24], F32, off); off += 128
    mv = sbuf("mv", [128, 2], F32, off); off += 64
    rt = sbuf("rt", [128, H], F32, off); off += 64
    s0 = sbuf("s0", [128, H], F32, off); off += 64
    tmpn = sbuf("tmpn", [128, H], F32, off); off += 64
    teps = sbuf("teps", [128, 1], F32, off); off += 64
    assert off <= R1

    xfmb = sbuf("xfmb", [128, KC, T], BF16, R1)
    qfm = sbuf("qfm", [128, KC, T], BF16, R1 + 32768)
    hfmb = sbuf("hfmb", [128, JC, TH], BF16, R1)
    ktm = sbuf("ktm", [128, NTC, D], BF16, R2)
    kvf = sbuf("kvf", [128, D], F32, R2)
    vtm = sbuf("vtm", [128, NTC, D], BF16, R2 + NTC * D * 2)
    x2h = sbuf("x2h", [128, NTCH, D], F32, R2)
    nuT = sbuf("nuT", [128, KC, T], BF16, R2 + 32768)
    p1T = sbuf("p1T", [128, 2, 16, 512], BF16, R2 + 32768)
    p2Tr = sbuf("p2Tr", [128, NPR, 512], BF16, R3 + 16384)
    outt = sbuf("outt", [128, D], F32, R3 + 24576)
    wstr = sbuf("wstr", [128, 2, 16, 512], BF16, R3)
    urld = sbuf("urld", [128, 2, D], F32, R3)
    x2fmb = sbuf("x2fmb", [128, KC, TH], BF16, R3)
    ubnc = sbuf("ubnc", [128, 4, 512], F32, R4)
    xres = sbuf("xres", [128, D], F32, R4)
    nut = sbuf("nut", [128, 2, D], BF16, R5)

    prog = {k: [] for k in ("tensor", "scalar", "vector", "sync", "gpsimd")}

    def em(eng, fn):
        prog[eng].append(fn)

    def wait(eng, sem, val):
        if val > 0:
            em(eng, lambda e, s=sem.h, v=val: e.wait_ge(s, v))

    def winc(eng, sem, n=1):
        t = sem.inc(n)
        last = prog[eng].pop()
        em(eng, lambda e, f=last, s=sem.h, m=n: f(e).then_inc(s, m))
        return t

    with ExitStack() as stack:
        def mksem(name):
            return Sem(stack.enter_context(nc.semaphore(name)))

        sTe, sSc, sVe = mksem("sTe"), mksem("sSc"), mksem("sVe")
        sCc = mksem("sCc")
        ENG_SEM = {"sc": sSc, "ve": sVe, "te": sTe}
        sXbf = mksem("sXbf")
        sCst = mksem("sCst")
        sFF = mksem("sFF")
        sKv1, sKv2 = mksem("sKv1"), mksem("sKv2")
        sW = [mksem("sW0"), mksem("sW1")]
        sUb = [mksem(f"sUb{i}") for i in range(4)]
        sUr = [mksem("sUr0"), mksem("sUr1")]
        sNu = mksem("sNu")
        sX2h = [mksem("sX2a"), mksem("sX2b")]
        sXf2 = [mksem("sXfa"), mksem("sXfb")]
        sXrl = [mksem("sXra"), mksem("sXrb")]
        sXr = mksem("sXr")
        sSt = mksem("sSt")
        sTx = mksem("sTx")
        sTn = mksem("sTn")
        sTf = mksem("sTf")
        sTp = [mksem("sTp0"), mksem("sTp1")]
        sP = [mksem(f"sP{i}") for i in range(NPR)]

        def gdma(sem, dst, src, **kw):
            t = sem.inc(16)
            em("gpsimd", lambda e, d=dst, s=src, k=kw:
               e.dma_start(out=d, in_=s, **k).then_inc(sem.h, 16))
            return t

        def ytr(sem, dst, src):
            t = sem.inc(16)
            em("sync", lambda e, d=dst, s=src:
               e.dma_start_transpose(d, s).then_inc(sem.h, 16))
            return t

        def str_(sem, dst, src):
            t = sem.inc(16)
            em("scalar", lambda e, d=dst, s=src:
               e.dma_start_transpose(d, s).then_inc(sem.h, 16))
            return t

        def sdma(sem, dst, src, **kw):
            t = sem.inc(16)
            em("scalar", lambda e, d=dst, s=src, k=kw:
               e.dma_start(out=d, in_=s, **k).then_inc(sem.h, 16))
            return t

        gdma(sXbf, x_bf, x)
        gdma(sCst, b2b[0:1, :], b2_h.ap()[None, :])
        em("vector", lambda e: e.memset(ones[:], 1.0))
        t_ones = winc("vector", sVe)

        wait("sync", sXbf, sXbf.v)
        for kc in range(KC):
            ytr(sTx, xfmb[:, kc, :], x_bf[:, kc * 128:(kc + 1) * 128])

        cast_chunks = []
        for i in range(32):
            cast_chunks.append((p1_bf[i * 256:(i + 1) * 256, :],
                                p1_h.ap()[i * 256:(i + 1) * 256, :]))
        for i in range(32):
            cast_chunks.append((p2_bf[i * 64:(i + 1) * 64, :],
                                p2_h.ap()[i * 64:(i + 1) * 64, :]))

        def drip(n=1):
            for _ in range(n):
                if cast_chunks:
                    d, s = cast_chunks.pop(0)
                    gdma(sFF, d, s)

        with (
            nc.psum_tensor("kvps", [128, D], F32) as kvps,
            nc.psum_tensor("pj0", [128, 512], F32) as pj0,
            nc.psum_tensor("pj1", [128, 512], F32) as pj1,
        ):
            pj = [pj0, pj1]

            state = {"oscnt": 0}
            slot_release = [None, None]
            load_done = {}

            def load_wtiles(proj, wap, os, slot):
                if slot_release[slot] is not None:
                    wait("gpsimd", sTe, slot_release[slot])
                for k in range(16):
                    gdma(sW[slot], wstr[:, slot, k, :],
                         wap[k * 128:(k + 1) * 128, os * 512:(os + 1) * 512])
                    if k in (5, 11, 15):
                        drip()
                load_done[(proj, os)] = (slot, sW[slot].v)

            def wait_wtiles(proj, os):
                slot, tot = load_done[(proj, os)]
                wait("tensor", sW[slot], tot)
                return slot

            pjcnt = [0]
            pj_release = {}

            def pj_guard(idx):
                if idx >= 2 and pj_release.get(idx - 2) is not None:
                    kind, tgt = pj_release[idx - 2]
                    wait("tensor", ENG_SEM[kind], tgt)

            def stream_proj_tm(proj, wap, evac):
                for os in range(4):
                    slot = state["oscnt"] % 2
                    load_wtiles(proj, wap, os, slot)
                    wait_wtiles(proj, os)
                    for tc in range(NTC):
                        idx = pjcnt[0]
                        ps = pj[idx % 2]
                        pj_guard(idx)
                        for k in range(16):
                            em("tensor", lambda e, p=ps,
                               l=xfmb[:, k, tc * 128:(tc + 1) * 128],
                               r=wstr[:, slot, k, :], st=(k == 0), sp=(k == 15):
                               e.matmul(p[:], l, r, start=st, stop=sp))
                        t = winc("tensor", sTe)
                        if tc == NTC - 1:
                            slot_release[slot] = t
                        pj_release[idx] = evac(os, tc, ps, t)
                        pjcnt[0] += 1
                    state["oscnt"] += 1

            def evac_k(os, tc, ps, t):
                wait("scalar", sTe, t)
                em("scalar", lambda e, o=ktm[:, tc, os * 512:(os + 1) * 512], p=ps:
                   e.activation(o, p[:], AF.Relu, scale=1.0 / SCALE))
                return ("sc", winc("scalar", sSc))

            def evac_v(os, tc, ps, t):
                wait("scalar", sTe, t)
                em("scalar", lambda e, o=vtm[:, tc, os * 512:(os + 1) * 512], p=ps:
                   e.copy(o, p[:]))
                return ("sc", winc("scalar", sSc))

            ucnt = [0]

            def evac_u(os, tc, ps, t):
                ui = ucnt[0] % 4
                wait("scalar", sTe, t)
                if ucnt[0] >= 4:
                    wait("scalar", sUb[ui], sUb[ui].v)
                em("scalar", lambda e, o=ubnc[:, ui, :], p=ps:
                   e.activation(o, p[:], AF.Silu))
                tsc = winc("scalar", sSc)
                wait("gpsimd", sSc, tsc)
                gdma(sUb[ui], u_dr[tc * 128:(tc + 1) * 128, os * 512:(os + 1) * 512],
                     ubnc[:, ui, :])
                ucnt[0] += 1
                return ("sc", tsc)

            wait("tensor", sTx, sTx.v)

            stream_proj_tm("k", w_ap["k"], evac_k)
            stream_proj_tm("v", w_ap["v"], evac_v)

            wait("tensor", sSc, sSc.v)
            for h in range(H):
                for tcc in range(NTC):
                    em("tensor", lambda e, o=kvps[:, h * 128:(h + 1) * 128],
                       l=ktm[:, tcc, h * 128:(h + 1) * 128],
                       r=vtm[:, tcc, h * 128:(h + 1) * 128],
                       st=(tcc == 0), sp=(tcc == NTC - 1):
                       e.matmul(o, l, r, start=st, stop=sp))
            t_kvmm = winc("tensor", sTe)
            wait("scalar", sTe, t_kvmm)
            em("scalar", lambda e: e.copy(kvf[:], kvps[:]))
            t_kvf = winc("scalar", sSc)
            wait("gpsimd", sSc, t_kvf)
            gdma(sKv1, kv_in, kvf[:])
            wait("gpsimd", sKv1, sKv1.v)
            em("gpsimd", lambda e: e.collective_compute(
                "AllReduce", OP.add,
                replica_groups=[[0, 1, 2, 3], [4, 5, 6, 7]],
                ins=[kv_in], outs=[kv_out]))
            t_cc = winc("gpsimd", sCc)
            wait("gpsimd", sCc, t_cc)
            gdma(sKv2, kvb[:], kv_out)

            stream_proj_tm("u", w_ap["u"], evac_u)

            for os in range(4):
                slot = state["oscnt"] % 2
                load_wtiles("q", w_ap["q"], os, slot)
                wait_wtiles("q", os)
                for oc in range(4):
                    for tt in range(NTT):
                        idx = pjcnt[0]
                        ps = pj[idx % 2]
                        pj_guard(idx)
                        for k in range(16):
                            em("tensor", lambda e, p=ps,
                               l=wstr[:, slot, k, oc * 128:(oc + 1) * 128],
                               r=xfmb[:, k, tt * TT:(tt + 1) * TT],
                               st=(k == 0), sp=(k == 15):
                               e.matmul(p[:, :TT], l, r, start=st, stop=sp))
                        t = winc("tensor", sTe)
                        if oc == 3 and tt == NTT - 1:
                            slot_release[slot] = t
                        ocg = os * 4 + oc
                        wait("scalar", sTe, t)
                        em("scalar", lambda e,
                           o=qfm[:, ocg, tt * TT:(tt + 1) * TT], p=ps:
                           e.activation(o, p[:, :TT], AF.Relu, scale=1.0 / SCALE))
                        pj_release[idx] = ("sc", winc("scalar", sSc))
                        pjcnt[0] += 1
                state["oscnt"] += 1
            for j in range(JC):
                gdma(sCst, b1s[:, j:j + 1], b1_h.ap()[j * 128:(j + 1) * 128, None])
            for back in (2, 1):
                r = pj_release.get(pjcnt[0] - back)
                if r is not None:
                    kind, tgt = r
                    wait("tensor", ENG_SEM[kind], tgt)

        t_q_evacs = sSc.v

        with (
            nc.psum_tensor("aops0", [128, D], F32) as aops0,
            nc.psum_tensor("aops1", [128, D], F32) as aops1,
        ):
            aops = [aops0, aops1]
            wait("tensor", sVe, sVe.v)
            wait("tensor", sSc, sSc.v)
            wait("gpsimd", sTe, sTe.v)
            for i in range(4):
                wait("gpsimd", sUb[i], sUb[i].v)
            wait("tensor", sKv2, sKv2.v)
            wait("tensor", sSc, t_q_evacs)
            nu_written = []
            aops_release = [None, None]
            urld_release = [None, None]
            prev_tnu = 0
            for tc in range(NTC):
                sl = tc % 2
                if urld_release[sl] is not None:
                    wait("gpsimd", sVe, urld_release[sl])
                gdma(sUr[sl], urld[:, sl, :], u_dr[tc * 128:(tc + 1) * 128, :])
                t_u = sUr[sl].v
                drip()
                if aops_release[sl] is not None:
                    wait("tensor", sVe, aops_release[sl])
                for h in range(H):
                    em("tensor", lambda e, o=aops[sl][:, h * 128:(h + 1) * 128],
                       l=qfm[:, h, tc * 128:(tc + 1) * 128],
                       r=kvb[:, h * 128:(h + 1) * 128]:
                       e.matmul(o, l, r, start=True, stop=True))
                t_ao = winc("tensor", sTe)
                wait("scalar", sTe, t_ao)
                if prev_tnu:
                    wait("scalar", sVe, prev_tnu)
                if tc >= 2:
                    wait("scalar", sNu, sNu.v)
                for h in range(H):
                    em("scalar", lambda e, o=nut[:, sl, h * 128:(h + 1) * 128],
                       p=aops[sl][:, h * 128:(h + 1) * 128], a=ssq[:, h:h + 1]:
                       e.activation(o, p, AF.Square, accum_out=a))
                t_sq = winc("scalar", sSc)
                wait("scalar", sSc, t_sq)
                em("scalar", lambda e: e.activation(rt[:], ssq[:], AF.Sqrt,
                                                    scale=1.0 / DH))
                t_rt = winc("scalar", sSc)
                wait("vector", sSc, t_rt)
                em("vector", lambda e: e.tensor_scalar_max(rt[:], rt[:], SRMS_EPS))
                t1 = winc("vector", sVe)
                wait("vector", sVe, t1)
                em("vector", lambda e: e.reciprocal(s0[:], rt[:]))
                t2 = winc("vector", sVe)
                wait("vector", sVe, t2)
                em("vector", lambda e: e.tensor_mul(tmpn[:], s0[:], s0[:]))
                t3 = winc("vector", sVe)
                wait("vector", sVe, t3)
                em("vector", lambda e: e.tensor_mul(tmpn[:], tmpn[:], ssq[:]))
                t4 = winc("vector", sVe)
                wait("vector", sVe, t4)
                em("vector", lambda e: e.tensor_scalar(tmpn[:], tmpn[:],
                                                       -0.5 / DH, 1.5, OP.mult, OP.add))
                t5 = winc("vector", sVe)
                wait("vector", sVe, t5)
                em("vector", lambda e: e.tensor_mul(s0[:], s0[:], tmpn[:]))
                t6 = winc("vector", sVe)
                wait("vector", sVe, t6)
                wait("vector", sUr[sl], t_u)
                for h in range(H):
                    em("vector", lambda e, o=nut[:, sl, h * 128:(h + 1) * 128],
                       p=aops[sl][:, h * 128:(h + 1) * 128], s=s0[:, h:h + 1],
                       u=urld[:, sl, h * 128:(h + 1) * 128]:
                       e.scalar_tensor_tensor(o, p, s, u, OP.mult, OP.mult))
                t_nu = winc("vector", sVe)
                prev_tnu = t_nu
                aops_release[sl] = t_nu
                urld_release[sl] = t_nu
                wait("gpsimd", sVe, t_nu)
                nu_written.append(gdma(sNu, nu_bf[tc * 128:(tc + 1) * 128, :],
                                       nut[:, sl, :]))

        wait("sync", sNu, sNu.v)
        for kc in range(KC):
            ytr(sTn, nuT[:, kc, :], nu_bf[:, kc * 128:(kc + 1) * 128])

        with (
            nc.psum_tensor("yps0", [128, 512], F32) as yps0,
            nc.psum_tensor("yps1", [128, 512], F32) as yps1,
        ):
            yps = [yps0, yps1]
            ycnt = 0
            y_release = {}
            xres_release = None
            x2h_free = None
            wait("tensor", sTn, sTn.v)
            wait("tensor", sVe, sVe.v)
            wait("tensor", sSc, sSc.v)
            wait("gpsimd", sVe, sVe.v)
            for hf in range(2):
                tc0 = hf * NTCH
                for os in range(4):
                    slot = state["oscnt"] % 2
                    load_wtiles(("o", hf), w_ap["o"], os, slot)
                    wait_wtiles(("o", hf), os)
                    for tc4 in range(NTCH):
                        tc = tc0 + tc4
                        ps = yps[ycnt % 2]
                        if ycnt >= 2 and y_release.get(ycnt - 2) is not None:
                            wait("tensor", sSc, y_release[ycnt - 2])
                        for k in range(16):
                            em("tensor", lambda e, p=ps,
                               l=nuT[:, k, tc * 128:(tc + 1) * 128],
                               r=wstr[:, slot, k, :], st=(k == 0), sp=(k == 15):
                               e.matmul(p[:], l, r, start=st, stop=sp))
                        t = winc("tensor", sTe)
                        if tc4 == NTCH - 1:
                            slot_release[slot] = t
                        wait("scalar", sTe, t)
                        if hf == 1 and os == 0 and x2h_free is not None:
                            wait("scalar", sX2h[0], sX2h[0].v)
                            wait("scalar", sXf2[0], sXf2[0].v)
                        em("scalar", lambda e,
                           o=x2h[:, tc4, os * 512:(os + 1) * 512],
                           p=ps: e.copy(o, p[:]))
                        y_release[ycnt] = winc("scalar", sSc)
                        ycnt += 1
                    state["oscnt"] += 1
                t_evac_half = sSc.v
                wait("vector", sSc, t_evac_half)
                for tc4 in range(NTCH):
                    tc = tc0 + tc4
                    if xres_release is not None:
                        wait("gpsimd", sVe, xres_release)
                    gdma(sXr, xres[:], x[tc * 128:(tc + 1) * 128, :])
                    drip()
                    wait("vector", sXr, sXr.v)
                    em("vector", lambda e, o=x2h[:, tc4, :]:
                       e.tensor_add(o, o, xres[:]))
                    t_add = winc("vector", sVe)
                    xres_release = t_add
                    wait("vector", sVe, t_add)
                    for i in range(4):
                        em("vector", lambda e, o=st6[:, 6 * i:6 * (i + 1)],
                           s=x2h[:, tc4, 512 * i:512 * (i + 1)]:
                           e.bn_stats(o, s))
                    t_st = winc("vector", sVe)
                    wait("vector", sVe, t_st)
                    em("vector", lambda e: e.bn_aggr(mv[:], st6[:]))
                    t_mv = winc("vector", sVe)
                    wait("vector", sVe, t_mv)
                    em("vector", lambda e: e.tensor_scalar_add(teps[:], mv[:, 1:2],
                                                               LN_EPS))
                    t2 = winc("vector", sVe)
                    wait("scalar", sVe, t2)
                    em("scalar", lambda e: e.activation(rt[:, 0:1], teps[:], AF.Sqrt))
                    t_rt2 = winc("scalar", sSc)
                    wait("vector", sSc, t_rt2)
                    em("vector", lambda e: e.reciprocal(s0[:, 0:1], rt[:, 0:1]))
                    t1 = winc("vector", sVe)
                    wait("vector", sVe, t1)
                    em("vector", lambda e: e.tensor_mul(tmpn[:, 0:1], s0[:, 0:1],
                                                        s0[:, 0:1]))
                    t3 = winc("vector", sVe)
                    wait("vector", sVe, t3)
                    em("vector", lambda e: e.tensor_mul(tmpn[:, 0:1], tmpn[:, 0:1],
                                                        teps[:]))
                    t4 = winc("vector", sVe)
                    wait("vector", sVe, t4)
                    em("vector", lambda e: e.tensor_scalar(tmpn[:, 0:1], tmpn[:, 0:1],
                                                           -0.5, 1.5, OP.mult, OP.add))
                    t5 = winc("vector", sVe)
                    wait("vector", sVe, t5)
                    em("vector", lambda e: e.tensor_mul(s0[:, 0:1], s0[:, 0:1],
                                                        tmpn[:, 0:1]))
                    t6 = winc("vector", sVe)
                    wait("vector", sVe, t6)
                    em("vector", lambda e, o=x2h[:, tc4, :]:
                       e.tensor_scalar(o, o, mv[:, 0:1], s0[:, 0:1],
                                       OP.subtract, OP.mult))
                    t_ln = winc("vector", sVe)
                    wait("gpsimd", sVe, t_ln)
                    gdma(sX2h[hf], x2_bf[tc * 128:(tc + 1) * 128, :], x2h[:, tc4, :])
                    gdma(sXf2[hf], x2f_dr[tc * 128:(tc + 1) * 128, :], x2h[:, tc4, :])
                    drip()
                x2h_free = True

        drip(len(cast_chunks))
        assert not cast_chunks

        with nc.psum_tensor("fps", [128, 8, 512], F32) as fps:
            p1T_release = [None, None]
            p2Tr_release = [None] * NPR
            bank_rel = {}
            out_stored = 0
            wait("sync", sTe, sTe.v)
            wait("sync", sFF, sFF.v)
            wait("scalar", sFF, sFF.v)
            wait("tensor", sSc, sSc.v)
            x2fmb_release = [None]
            ln2_done = 0
            for hf in range(2):
                tc0 = hf * NTCH
                wait("sync", sX2h[hf], sX2h[hf].v)
                if hf == 1:
                    wait("sync", sTe, x2fmb_release[0])
                    wait("sync", sVe, sVe.v)
                    wait("sync", sSc, sSc.v)
                    wait("sync", sTe, sTe.v)
                for kc in range(KC):
                    ytr(sTf, x2fmb[:, kc, :], x2_bf[hf * TH:(hf + 1) * TH,
                                                    kc * 128:(kc + 1) * 128])
                t_x2f = sTf.v
                x2fmb_release = [None]
                h_done = {}
                wait("tensor", sTf, t_x2f)
                wait("scalar", sCst, sCst.v)
                for jg in range(16):
                    sl = jg % 2
                    if p1T_release[sl] is not None:
                        wait("sync", sTe, p1T_release[sl])
                    for e_ in range(16):
                        ytr(sTp[sl], p1T[:, sl, e_, :],
                            p1_bf[jg * 512:(jg + 1) * 512, e_ * 128:(e_ + 1) * 128])
                    wait("tensor", sTp[sl], sTp[sl].v)
                    for j4 in range(4):
                        jc = jg * 4 + j4
                        bank = jc % 8
                        if bank_rel.get(bank) is not None:
                            kind, tgt = bank_rel[bank]
                            wait("tensor", ENG_SEM[kind], tgt)
                        for e_ in range(16):
                            em("tensor", lambda e, o=fps[:, bank, :TH],
                               l=p1T[:, sl, e_, j4 * 128:(j4 + 1) * 128],
                               r=x2fmb[:, e_, :], st=(e_ == 0), sp=(e_ == 15):
                               e.matmul(o, l, r, start=st, stop=sp))
                        t = winc("tensor", sTe)
                        if j4 == 3:
                            p1T_release[sl] = t
                        if jg == 15 and j4 == 3:
                            x2fmb_release[0] = t
                        wait("scalar", sTe, t)
                        em("scalar", lambda e, o=hfmb[:, jc, :], p=fps[:, bank, :TH],
                           b=b1s[:, jc:jc + 1]:
                           e.activation(o, p, AF.Gelu, bias=b))
                        h_done[jc] = winc("scalar", sSc)
                        bank_rel[bank] = ("sc", h_done[jc])
                for s_ in (sXf2[0], sXf2[1], sX2h[0], sX2h[1]):
                    wait("gpsimd", s_, s_.v)
                if hf == 1:
                    wait("gpsimd", sVe, ln2_done)
                for tc4 in range(NTCH):
                    tc = tc0 + tc4
                    gdma(sXrl[hf], x2h[:, tc4, :], x2f_dr[tc * 128:(tc + 1) * 128, :])
                wait("tensor", sCst, sCst.v)
                wait("vector", sVe, t_ones)
                wait("vector", sXrl[hf], sXrl[hf].v)
                es_evac = {}
                for es in range(4):
                    for tc4 in range(NTCH):
                        bank = tc4
                        if bank_rel.get(bank) is not None:
                            kind, tgt = bank_rel[bank]
                            wait("tensor", ENG_SEM[kind], tgt)
                        em("tensor", lambda e, o=fps[:, bank, :],
                           r=b2b[0:1, es * 512:(es + 1) * 512]:
                           e.matmul(o, ones[0:1, :], r, start=True, stop=False))
                    for j in range(JC):
                        rsl = j % NPR
                        if es == 0:
                            wait("tensor", sSc, h_done[j])
                        if p2Tr_release[rsl] is not None:
                            wait("scalar", sTe, p2Tr_release[rsl])
                        str_(sP[rsl], p2Tr[:, rsl, :],
                             p2_bf[es * 512:(es + 1) * 512, j * 128:(j + 1) * 128])
                        wait("tensor", sP[rsl], sP[rsl].v)
                        for tc4 in range(NTCH):
                            em("tensor", lambda e, o=fps[:, tc4, :],
                               l=hfmb[:, j, tc4 * 128:(tc4 + 1) * 128],
                               r=p2Tr[:, rsl, :], sp=(j == JC - 1):
                               e.matmul(o, l, r, start=False, stop=sp))
                        p2Tr_release[rsl] = winc("tensor", sTe)
                    t_es = sTe.v
                    for tc4 in range(NTCH):
                        wait("vector", sTe, t_es)
                        em("vector", lambda e,
                           o=x2h[:, tc4, es * 512:(es + 1) * 512],
                           p=fps[:, tc4, :]: e.tensor_add(o, o, p))
                        es_evac[(es, tc4)] = winc("vector", sVe)
                        bank_rel[tc4] = ("ve", es_evac[(es, tc4)])
                for tc4 in range(NTCH):
                    tc = tc0 + tc4
                    wait("vector", sVe, es_evac[(3, tc4)])
                    for i in range(4):
                        em("vector", lambda e, o=st6[:, 6 * i:6 * (i + 1)],
                           s=x2h[:, tc4, 512 * i:512 * (i + 1)]:
                           e.bn_stats(o, s))
                    t_st = winc("vector", sVe)
                    wait("vector", sVe, t_st)
                    em("vector", lambda e: e.bn_aggr(mv[:], st6[:]))
                    t_mv = winc("vector", sVe)
                    wait("vector", sVe, t_mv)
                    em("vector", lambda e: e.tensor_scalar_add(teps[:], mv[:, 1:2],
                                                               LN_EPS))
                    t2 = winc("vector", sVe)
                    wait("scalar", sVe, t2)
                    em("scalar", lambda e: e.activation(rt[:, 0:1], teps[:], AF.Sqrt))
                    t_rt2 = winc("scalar", sSc)
                    wait("vector", sSc, t_rt2)
                    em("vector", lambda e: e.reciprocal(s0[:, 0:1], rt[:, 0:1]))
                    t1 = winc("vector", sVe)
                    wait("vector", sVe, t1)
                    em("vector", lambda e: e.tensor_mul(tmpn[:, 0:1], s0[:, 0:1],
                                                        s0[:, 0:1]))
                    t3 = winc("vector", sVe)
                    wait("vector", sVe, t3)
                    em("vector", lambda e: e.tensor_mul(tmpn[:, 0:1], tmpn[:, 0:1],
                                                        teps[:]))
                    t4 = winc("vector", sVe)
                    wait("vector", sVe, t4)
                    em("vector", lambda e: e.tensor_scalar(tmpn[:, 0:1], tmpn[:, 0:1],
                                                           -0.5, 1.5, OP.mult, OP.add))
                    t5 = winc("vector", sVe)
                    wait("vector", sVe, t5)
                    em("vector", lambda e: e.tensor_mul(s0[:, 0:1], s0[:, 0:1],
                                                        tmpn[:, 0:1]))
                    t6 = winc("vector", sVe)
                    wait("vector", sVe, t6)
                    if out_stored:
                        wait("vector", sSt, sSt.v)
                    em("vector", lambda e, s=x2h[:, tc4, :]:
                       e.tensor_scalar(outt[:], s, mv[:, 0:1], s0[:, 0:1],
                                       OP.subtract, OP.mult))
                    t_o = winc("vector", sVe)
                    ln2_done = t_o
                    wait("scalar", sVe, t_o)
                    sdma(sSt, out[tc * 128:(tc + 1) * 128, :], outt[:])
                    out_stored += 1

        if taps:
            tapable = {"x_bf": (x_bf, [T, D], BF16), "u_dr": (u_dr, [T, D], F32),
                       "kv_out": (kv_out, [128, D], F32), "nu_bf": (nu_bf, [T, D], BF16),
                       "x2_bf": (x2_bf, [T, D], BF16), "x2f_dr": (x2f_dr, [T, D], F32),
                       "p1_bf": (p1_bf, [HID, D], BF16), "p2_bf": (p2_bf, [D, HID], BF16),
                       "sb_hfmb": (hfmb[:, :, :], [128, JC * TH], BF16),
                       "sb_x2fmb": (x2fmb[:, :, :], [128, KC * TH], BF16),
                       "sb_x2h": (x2h[:, :, :], [128, NTCH * D], F32),
                       "sb_p1T": (p1T[:, :, :, :], [128, 2 * 16 * 512], BF16)}
            sTap = mksem("sTap")
            for s in ([sXbf, sCst, sFF, sKv1, sKv2, sNu, sXr, sSt, sTx, sTn, sTf]
                      + sW + sUb + sUr + sX2h + sXf2 + sXrl + sTp + sP):
                wait("sync", s, s.v)
            for name in taps:
                ap, shp, dt = tapable[name]
                if name.startswith("sb_"):
                    shp = [128, shp[1]] if len(shp) == 2 else shp
                th = nc.dram_tensor("tap_" + name, shp, dt, kind="ExternalOutput")
                t = sTap.inc(16)
                em("sync", lambda e, d=th.ap(), s=ap: e.dma_start(out=d, in_=s).then_inc(sTap.h, 16))
            wait("sync", sTap, sTap.v)

        wait("sync", sSt, sSt.v)
        for s in ([sXbf, sCst, sFF, sKv1, sKv2, sNu, sXr, sTx, sTn, sTf]
                  + sW + sUb + sUr + sX2h + sXf2 + sXrl + sTp + sP):
            wait("sync", s, s.v)

        with nc.Block() as block:
            @block.tensor
            def _(e):
                for f in prog["tensor"]:
                    f(e)

            @block.scalar
            def _(e):
                for f in prog["scalar"]:
                    f(e)

            @block.vector
            def _(e):
                for f in prog["vector"]:
                    f(e)

            @block.sync
            def _(e):
                for f in prog["sync"]:
                    f(e)

            @block.gpsimd
            def _(e):
                for f in prog["gpsimd"]:
                    f(e)

    return nc


_NC_CACHE = {}


def kernel(L0_attn_x, L0_attn_Wq, L0_attn_Wk, L0_attn_Wv, L0_attn_Wu,
           L0_attn_Wo, L0_ffn_proj1, L0_ffn_proj1_bias, L0_ffn_proj2,
           L0_ffn_proj2_bias, L0_ffn_gate_w):
    T = 1024
    x = np.ascontiguousarray(np.asarray(L0_attn_x, dtype=np.float32))
    B, L, _ = x.shape
    per = {
        "wq": np.ascontiguousarray(np.asarray(L0_attn_Wq, dtype=np.float32)),
        "wk": np.ascontiguousarray(np.asarray(L0_attn_Wk, dtype=np.float32)),
        "wv": np.ascontiguousarray(np.asarray(L0_attn_Wv, dtype=np.float32)),
        "wu": np.ascontiguousarray(np.asarray(L0_attn_Wu, dtype=np.float32)),
        "wo": np.ascontiguousarray(np.asarray(L0_attn_Wo, dtype=np.float32)),
        "p1": np.ascontiguousarray(np.asarray(L0_ffn_proj1, dtype=np.float32)),
        "b1": np.ascontiguousarray(np.asarray(L0_ffn_proj1_bias, dtype=np.float32)),
        "p2": np.ascontiguousarray(np.asarray(L0_ffn_proj2, dtype=np.float32)),
        "b2": np.ascontiguousarray(np.asarray(L0_ffn_proj2_bias, dtype=np.float32)),
    }
    if T not in _NC_CACHE:
        _NC_CACHE[T] = build_nc(T)
    nc = _NC_CACHE[T]
    in_maps = []
    for c in range(8):
        b = c // 4
        t0 = (c % 4) * T
        m = dict(per)
        m["x"] = np.ascontiguousarray(x[b, t0:t0 + T])
        in_maps.append(m)
    res = run_bass_kernel_spmd(nc, in_maps, core_ids=list(range(8)))
    full = np.empty((B, L, D), dtype=np.float32)
    for c in range(8):
        b = c // 4
        t0 = (c % 4) * T
        full[b, t0:t0 + T] = res.results[c]["out"]
    return full


# version 05
# speedup vs baseline: 1.1100x; 1.0739x over previous
import math
from contextlib import ExitStack

import numpy as np

import concourse.bass as bass
import concourse.mybir as mybir
from concourse.bass_utils import run_bass_kernel_spmd

F32 = mybir.dt.float32
BF16 = mybir.dt.bfloat16
I8 = mybir.dt.int8
AF = mybir.ActivationFunctionType
OP = mybir.AluOpType

D = 2048
KC = D // 128
H = 16
DH = 128
HID = 8192
JC = HID // 128
SCALE = math.sqrt(DH)
LN_EPS = 1e-5
SRMS_EPS = 1e-12
NPR = 8


class Sem:
    def __init__(self, h):
        self.h = h
        self.v = 0

    def inc(self, n):
        self.v += n
        return self.v


def build_nc(T=1024, taps=()):
    NTC = T // 128
    TT = min(512, T)
    NTT = T // TT
    TH = T // 2
    NTCH = TH // 128

    nc = bass.Bass("TRN2", target_bir_lowering=False)

    x_h = nc.dram_tensor("x", [T, D], F32, kind="ExternalInput")
    wq_h = nc.dram_tensor("wq", [D, D], F32, kind="ExternalInput")
    wk_h = nc.dram_tensor("wk", [D, D], F32, kind="ExternalInput")
    wv_h = nc.dram_tensor("wv", [D, D], F32, kind="ExternalInput")
    wu_h = nc.dram_tensor("wu", [D, D], F32, kind="ExternalInput")
    wo_h = nc.dram_tensor("wo", [D, D], F32, kind="ExternalInput")
    p1_h = nc.dram_tensor("p1", [HID, D], F32, kind="ExternalInput")
    b1_h = nc.dram_tensor("b1", [HID], F32, kind="ExternalInput")
    p2_h = nc.dram_tensor("p2", [D, HID], F32, kind="ExternalInput")
    b2_h = nc.dram_tensor("b2", [D], F32, kind="ExternalInput")
    out_h = nc.dram_tensor("out", [T, D], F32, kind="ExternalOutput")

    x_bf = nc.dram_tensor("x_bf", [T, D], BF16).ap()
    nu_bf = nc.dram_tensor("nu_bf", [T, D], BF16).ap()
    x2_bf = nc.dram_tensor("x2_bf", [T, D], BF16).ap()
    x2f_dr = nc.dram_tensor("x2f_dr", [T, D], F32).ap()
    p1_bf = nc.dram_tensor("p1_bf", [HID, D], BF16).ap()
    p2_bf = nc.dram_tensor("p2_bf", [D, HID], BF16).ap()
    u_dr = nc.dram_tensor("u_dr", [T, D], F32).ap()
    kv_in = nc.dram_tensor("kv_in", [128, D], F32).ap()
    kv_out = nc.dram_tensor("kv_out", [128, D], F32).ap()

    x = x_h.ap()
    out = out_h.ap()
    w_ap = {"q": wq_h.ap(), "k": wk_h.ap(), "v": wv_h.ap(),
            "u": wu_h.ap(), "o": wo_h.ap()}

    R1 = 9216
    R2 = R1 + 65536
    R3 = R2 + 65536
    R4 = R3 + 32768
    R5 = R4 + 8192
    ARENA = R5 + 8192
    nc.alloc_sbuf_tensor("arena", [128, ARENA], I8)
    base = nc.lookup_mloc("arena").addr

    def sbuf(name, shape, dtype, off):
        return nc.alloc_sbuf_tensor_at(name, shape, dtype, offset=base + off)

    off = 0
    kvb = sbuf("kvb", [128, D], BF16, off); off += 4096
    b2b = sbuf("b2b", [1, D], BF16, off); off += 4096
    b1s = sbuf("b1s", [128, JC], F32, off); off += 256
    ones = sbuf("ones", [1, 128], BF16, off); off += 256
    ssq = sbuf("ssq", [128, H], F32, off); off += 64
    st6 = sbuf("st6", [128, 24], F32, off); off += 128
    mv = sbuf("mv", [128, 2], F32, off); off += 64
    rt = sbuf("rt", [128, H], F32, off); off += 64
    s0 = sbuf("s0", [128, H], F32, off); off += 64
    tmpn = sbuf("tmpn", [128, H], F32, off); off += 64
    teps = sbuf("teps", [128, 1], F32, off); off += 64
    assert off <= R1

    xfmb = sbuf("xfmb", [128, KC, T], BF16, R1)
    qfm = sbuf("qfm", [128, KC, T], BF16, R1 + 32768)
    hfmb = sbuf("hfmb", [128, JC, TH], BF16, R1)
    ktm = sbuf("ktm", [128, NTC, D], BF16, R2)
    kvf = sbuf("kvf", [128, D], F32, R2)
    vtm = sbuf("vtm", [128, NTC, D], BF16, R2 + NTC * D * 2)
    x2h = sbuf("x2h", [128, NTCH, D], F32, R2)
    nuT = sbuf("nuT", [128, KC, T], BF16, R2 + 32768)
    p1T = sbuf("p1T", [128, 2, 16, 512], BF16, R2 + 32768)
    p2Tr = sbuf("p2Tr", [128, NPR, 512], BF16, R3 + 16384)
    outt = sbuf("outt", [128, D], F32, R3 + 24576)
    wstr = sbuf("wstr", [128, 2, 16, 512], BF16, R3)
    urld = sbuf("urld", [128, 2, D], F32, R3)
    x2fmb = sbuf("x2fmb", [128, KC, TH], BF16, R3)
    ubnc = sbuf("ubnc", [128, 4, 512], F32, R4)
    xres = sbuf("xres", [128, D], F32, R4)
    nut = sbuf("nut", [128, 2, D], BF16, R5)

    prog = {k: [] for k in ("tensor", "scalar", "vector", "sync", "gpsimd")}

    def em(eng, fn):
        prog[eng].append(fn)

    def wait(eng, sem, val):
        if val > 0:
            em(eng, lambda e, s=sem.h, v=val: e.wait_ge(s, v))

    def winc(eng, sem, n=1):
        t = sem.inc(n)
        last = prog[eng].pop()
        em(eng, lambda e, f=last, s=sem.h, m=n: f(e).then_inc(s, m))
        return t

    with ExitStack() as stack:
        def mksem(name):
            return Sem(stack.enter_context(nc.semaphore(name)))

        sTe, sSc, sVe = mksem("sTe"), mksem("sSc"), mksem("sVe")
        sCc = mksem("sCc")
        ENG_SEM = {"sc": sSc, "ve": sVe, "te": sTe}
        sXbf = mksem("sXbf")
        sCst = mksem("sCst")
        sFF = mksem("sFF")
        sKv1, sKv2 = mksem("sKv1"), mksem("sKv2")
        sW = [mksem("sW0"), mksem("sW1")]
        sUb = [mksem(f"sUb{i}") for i in range(4)]
        sUr = [mksem("sUr0"), mksem("sUr1")]
        sNu = mksem("sNu")
        sX2h = [mksem("sX2a"), mksem("sX2b")]
        sXf2 = [mksem("sXfa"), mksem("sXfb")]
        sXrl = [mksem("sXra"), mksem("sXrb")]
        sXr = mksem("sXr")
        sSt = mksem("sSt")
        sTx = mksem("sTx")
        sTn = mksem("sTn")
        sTf = mksem("sTf")
        sTp = [mksem("sTp0"), mksem("sTp1")]
        sP = [mksem(f"sP{i}") for i in range(NPR)]

        def gdma(sem, dst, src, **kw):
            t = sem.inc(16)
            em("gpsimd", lambda e, d=dst, s=src, k=kw:
               e.dma_start(out=d, in_=s, **k).then_inc(sem.h, 16))
            return t

        def ytr(sem, dst, src):
            t = sem.inc(16)
            em("sync", lambda e, d=dst, s=src:
               e.dma_start_transpose(d, s).then_inc(sem.h, 16))
            return t

        def str_(sem, dst, src):
            t = sem.inc(16)
            em("scalar", lambda e, d=dst, s=src:
               e.dma_start_transpose(d, s).then_inc(sem.h, 16))
            return t

        def sdma(sem, dst, src, **kw):
            t = sem.inc(16)
            em("scalar", lambda e, d=dst, s=src, k=kw:
               e.dma_start(out=d, in_=s, **k).then_inc(sem.h, 16))
            return t

        gdma(sXbf, x_bf, x)
        gdma(sCst, b2b[0:1, :], b2_h.ap()[None, :])
        em("vector", lambda e: e.memset(ones[:], 1.0))
        t_ones = winc("vector", sVe)

        wait("sync", sXbf, sXbf.v)
        for kc in range(KC):
            ytr(sTx, xfmb[:, kc, :], x_bf[:, kc * 128:(kc + 1) * 128])

        cast_chunks = []
        for i in range(32):
            cast_chunks.append((p1_bf[i * 256:(i + 1) * 256, :],
                                p1_h.ap()[i * 256:(i + 1) * 256, :]))
        for i in range(32):
            cast_chunks.append((p2_bf[i * 64:(i + 1) * 64, :],
                                p2_h.ap()[i * 64:(i + 1) * 64, :]))

        def drip(n=1):
            for _ in range(n):
                if cast_chunks:
                    d, s = cast_chunks.pop(0)
                    gdma(sFF, d, s)

        with (
            nc.psum_tensor("kvps", [128, D], F32) as kvps,
            nc.psum_tensor("pj0", [128, 512], F32) as pj0,
            nc.psum_tensor("pj1", [128, 512], F32) as pj1,
        ):
            pj = [pj0, pj1]

            state = {"oscnt": 0}
            slot_release = [None, None]
            load_done = {}

            def load_wtiles(proj, wap, os, slot):
                if slot_release[slot] is not None:
                    wait("gpsimd", sTe, slot_release[slot])
                for k in range(16):
                    gdma(sW[slot], wstr[:, slot, k, :],
                         wap[k * 128:(k + 1) * 128, os * 512:(os + 1) * 512])
                    if proj in ("k", "v") and k in (3, 7, 11, 15):
                        drip()
                    elif proj == "q" and k in (5, 11, 15):
                        drip()
                load_done[(proj, os)] = (slot, sW[slot].v)

            def wait_wtiles(proj, os):
                slot, tot = load_done[(proj, os)]
                wait("tensor", sW[slot], tot)
                return slot

            pjcnt = [0]
            pj_release = {}

            def pj_guard(idx):
                if idx >= 2 and pj_release.get(idx - 2) is not None:
                    kind, tgt = pj_release[idx - 2]
                    wait("tensor", ENG_SEM[kind], tgt)

            def stream_proj_tm(proj, wap, evac):
                for os in range(4):
                    slot = state["oscnt"] % 2
                    load_wtiles(proj, wap, os, slot)
                    wait_wtiles(proj, os)
                    for tc in range(NTC):
                        idx = pjcnt[0]
                        ps = pj[idx % 2]
                        pj_guard(idx)
                        for k in range(16):
                            em("tensor", lambda e, p=ps,
                               l=xfmb[:, k, tc * 128:(tc + 1) * 128],
                               r=wstr[:, slot, k, :], st=(k == 0), sp=(k == 15):
                               e.matmul(p[:], l, r, start=st, stop=sp))
                        t = winc("tensor", sTe)
                        if tc == NTC - 1:
                            slot_release[slot] = t
                        pj_release[idx] = evac(os, tc, ps, t)
                        pjcnt[0] += 1
                    state["oscnt"] += 1

            def evac_k(os, tc, ps, t):
                wait("scalar", sTe, t)
                em("scalar", lambda e, o=ktm[:, tc, os * 512:(os + 1) * 512], p=ps:
                   e.activation(o, p[:], AF.Relu, scale=1.0 / SCALE))
                return ("sc", winc("scalar", sSc))

            def evac_v(os, tc, ps, t):
                wait("scalar", sTe, t)
                em("scalar", lambda e, o=vtm[:, tc, os * 512:(os + 1) * 512], p=ps:
                   e.copy(o, p[:]))
                return ("sc", winc("scalar", sSc))

            ucnt = [0]

            def evac_u(os, tc, ps, t):
                ui = ucnt[0] % 4
                wait("scalar", sTe, t)
                if ucnt[0] >= 4:
                    wait("scalar", sUb[ui], sUb[ui].v)
                em("scalar", lambda e, o=ubnc[:, ui, :], p=ps:
                   e.activation(o, p[:], AF.Silu))
                tsc = winc("scalar", sSc)
                wait("scalar", sSc, tsc)
                sdma(sUb[ui], u_dr[tc * 128:(tc + 1) * 128, os * 512:(os + 1) * 512],
                     ubnc[:, ui, :])
                ucnt[0] += 1
                return ("sc", tsc)

            wait("tensor", sTx, sTx.v)

            stream_proj_tm("k", w_ap["k"], evac_k)
            stream_proj_tm("v", w_ap["v"], evac_v)

            wait("tensor", sSc, sSc.v)
            for h in range(H):
                for tcc in range(NTC):
                    em("tensor", lambda e, o=kvps[:, h * 128:(h + 1) * 128],
                       l=ktm[:, tcc, h * 128:(h + 1) * 128],
                       r=vtm[:, tcc, h * 128:(h + 1) * 128],
                       st=(tcc == 0), sp=(tcc == NTC - 1):
                       e.matmul(o, l, r, start=st, stop=sp))
            t_kvmm = winc("tensor", sTe)
            wait("scalar", sTe, t_kvmm)
            em("scalar", lambda e: e.copy(kvf[:], kvps[:]))
            t_kvf = winc("scalar", sSc)
            wait("gpsimd", sSc, t_kvf)
            gdma(sKv1, kv_in, kvf[:])
            wait("gpsimd", sKv1, sKv1.v)
            em("gpsimd", lambda e: e.collective_compute(
                "AllReduce", OP.add,
                replica_groups=[[0, 1, 2, 3], [4, 5, 6, 7]],
                ins=[kv_in], outs=[kv_out]))
            t_cc = winc("gpsimd", sCc)

            stream_proj_tm("u", w_ap["u"], evac_u)

            for os in range(4):
                slot = state["oscnt"] % 2
                load_wtiles("q", w_ap["q"], os, slot)
                wait_wtiles("q", os)
                for oc in range(4):
                    for tt in range(NTT):
                        idx = pjcnt[0]
                        ps = pj[idx % 2]
                        pj_guard(idx)
                        for k in range(16):
                            em("tensor", lambda e, p=ps,
                               l=wstr[:, slot, k, oc * 128:(oc + 1) * 128],
                               r=xfmb[:, k, tt * TT:(tt + 1) * TT],
                               st=(k == 0), sp=(k == 15):
                               e.matmul(p[:, :TT], l, r, start=st, stop=sp))
                        t = winc("tensor", sTe)
                        if oc == 3 and tt == NTT - 1:
                            slot_release[slot] = t
                        ocg = os * 4 + oc
                        wait("scalar", sTe, t)
                        em("scalar", lambda e,
                           o=qfm[:, ocg, tt * TT:(tt + 1) * TT], p=ps:
                           e.activation(o, p[:, :TT], AF.Relu, scale=1.0 / SCALE))
                        pj_release[idx] = ("sc", winc("scalar", sSc))
                        pjcnt[0] += 1
                state["oscnt"] += 1
            wait("gpsimd", sCc, t_cc)
            gdma(sKv2, kvb[:], kv_out)
            for j in range(JC):
                gdma(sCst, b1s[:, j:j + 1], b1_h.ap()[j * 128:(j + 1) * 128, None])
            for back in (2, 1):
                r = pj_release.get(pjcnt[0] - back)
                if r is not None:
                    kind, tgt = r
                    wait("tensor", ENG_SEM[kind], tgt)

        t_q_evacs = sSc.v

        with (
            nc.psum_tensor("aops0", [128, D], F32) as aops0,
            nc.psum_tensor("aops1", [128, D], F32) as aops1,
        ):
            aops = [aops0, aops1]
            wait("tensor", sVe, sVe.v)
            wait("tensor", sSc, sSc.v)
            wait("gpsimd", sTe, sTe.v)
            for i in range(4):
                wait("gpsimd", sUb[i], sUb[i].v)
            wait("tensor", sKv2, sKv2.v)
            wait("tensor", sSc, t_q_evacs)
            nu_written = []
            aops_release = [None, None]
            urld_release = [None, None]
            prev_tnu = 0
            for tc in range(NTC):
                sl = tc % 2
                if urld_release[sl] is not None:
                    wait("gpsimd", sVe, urld_release[sl])
                gdma(sUr[sl], urld[:, sl, :], u_dr[tc * 128:(tc + 1) * 128, :])
                t_u = sUr[sl].v
                drip()
                if aops_release[sl] is not None:
                    wait("tensor", sVe, aops_release[sl])
                for h in range(H):
                    em("tensor", lambda e, o=aops[sl][:, h * 128:(h + 1) * 128],
                       l=qfm[:, h, tc * 128:(tc + 1) * 128],
                       r=kvb[:, h * 128:(h + 1) * 128]:
                       e.matmul(o, l, r, start=True, stop=True))
                t_ao = winc("tensor", sTe)
                wait("scalar", sTe, t_ao)
                if prev_tnu:
                    wait("scalar", sVe, prev_tnu)
                if tc >= 2:
                    wait("scalar", sNu, sNu.v)
                for h in range(H):
                    em("scalar", lambda e, o=nut[:, sl, h * 128:(h + 1) * 128],
                       p=aops[sl][:, h * 128:(h + 1) * 128], a=ssq[:, h:h + 1]:
                       e.activation(o, p, AF.Square, accum_out=a))
                t_sq = winc("scalar", sSc)
                wait("scalar", sSc, t_sq)
                em("scalar", lambda e: e.activation(rt[:], ssq[:], AF.Sqrt,
                                                    scale=1.0 / DH))
                t_rt = winc("scalar", sSc)
                wait("vector", sSc, t_rt)
                em("vector", lambda e: e.tensor_scalar_max(rt[:], rt[:], SRMS_EPS))
                t1 = winc("vector", sVe)
                wait("vector", sVe, t1)
                em("vector", lambda e: e.reciprocal(s0[:], rt[:]))
                t2 = winc("vector", sVe)
                wait("vector", sVe, t2)
                em("vector", lambda e: e.tensor_mul(tmpn[:], s0[:], s0[:]))
                t3 = winc("vector", sVe)
                wait("vector", sVe, t3)
                em("vector", lambda e: e.tensor_mul(tmpn[:], tmpn[:], ssq[:]))
                t4 = winc("vector", sVe)
                wait("vector", sVe, t4)
                em("vector", lambda e: e.tensor_scalar(tmpn[:], tmpn[:],
                                                       -0.5 / DH, 1.5, OP.mult, OP.add))
                t5 = winc("vector", sVe)
                wait("vector", sVe, t5)
                em("vector", lambda e: e.tensor_mul(s0[:], s0[:], tmpn[:]))
                t6 = winc("vector", sVe)
                wait("vector", sVe, t6)
                wait("vector", sUr[sl], t_u)
                for h in range(H):
                    em("vector", lambda e, o=nut[:, sl, h * 128:(h + 1) * 128],
                       p=aops[sl][:, h * 128:(h + 1) * 128], s=s0[:, h:h + 1],
                       u=urld[:, sl, h * 128:(h + 1) * 128]:
                       e.scalar_tensor_tensor(o, p, s, u, OP.mult, OP.mult))
                t_nu = winc("vector", sVe)
                prev_tnu = t_nu
                aops_release[sl] = t_nu
                urld_release[sl] = t_nu
                wait("gpsimd", sVe, t_nu)
                nu_written.append(gdma(sNu, nu_bf[tc * 128:(tc + 1) * 128, :],
                                       nut[:, sl, :]))

        wait("sync", sNu, sNu.v)
        for kc in range(KC):
            ytr(sTn, nuT[:, kc, :], nu_bf[:, kc * 128:(kc + 1) * 128])

        with (
            nc.psum_tensor("yps0", [128, 512], F32) as yps0,
            nc.psum_tensor("yps1", [128, 512], F32) as yps1,
        ):
            yps = [yps0, yps1]
            ycnt = 0
            y_release = {}
            xres_release = None
            x2h_free = None
            wait("tensor", sTn, sTn.v)
            wait("tensor", sVe, sVe.v)
            wait("tensor", sSc, sSc.v)
            wait("gpsimd", sVe, sVe.v)
            for hf in range(2):
                tc0 = hf * NTCH
                for os in range(4):
                    slot = state["oscnt"] % 2
                    load_wtiles(("o", hf), w_ap["o"], os, slot)
                    wait_wtiles(("o", hf), os)
                    for tc4 in range(NTCH):
                        tc = tc0 + tc4
                        ps = yps[ycnt % 2]
                        if ycnt >= 2 and y_release.get(ycnt - 2) is not None:
                            wait("tensor", sSc, y_release[ycnt - 2])
                        for k in range(16):
                            em("tensor", lambda e, p=ps,
                               l=nuT[:, k, tc * 128:(tc + 1) * 128],
                               r=wstr[:, slot, k, :], st=(k == 0), sp=(k == 15):
                               e.matmul(p[:], l, r, start=st, stop=sp))
                        t = winc("tensor", sTe)
                        if tc4 == NTCH - 1:
                            slot_release[slot] = t
                        wait("scalar", sTe, t)
                        if hf == 1 and os == 0 and x2h_free is not None:
                            wait("scalar", sX2h[0], sX2h[0].v)
                            wait("scalar", sXf2[0], sXf2[0].v)
                        em("scalar", lambda e,
                           o=x2h[:, tc4, os * 512:(os + 1) * 512],
                           p=ps: e.copy(o, p[:]))
                        y_release[ycnt] = winc("scalar", sSc)
                        ycnt += 1
                    state["oscnt"] += 1
                t_evac_half = sSc.v
                wait("vector", sSc, t_evac_half)
                for tc4 in range(NTCH):
                    tc = tc0 + tc4
                    if xres_release is not None:
                        wait("gpsimd", sVe, xres_release)
                    gdma(sXr, xres[:], x[tc * 128:(tc + 1) * 128, :])
                    drip()
                    wait("vector", sXr, sXr.v)
                    em("vector", lambda e, o=x2h[:, tc4, :]:
                       e.tensor_add(o, o, xres[:]))
                    t_add = winc("vector", sVe)
                    xres_release = t_add
                    wait("vector", sVe, t_add)
                    for i in range(4):
                        em("vector", lambda e, o=st6[:, 6 * i:6 * (i + 1)],
                           s=x2h[:, tc4, 512 * i:512 * (i + 1)]:
                           e.bn_stats(o, s))
                    t_st = winc("vector", sVe)
                    wait("vector", sVe, t_st)
                    em("vector", lambda e: e.bn_aggr(mv[:], st6[:]))
                    t_mv = winc("vector", sVe)
                    wait("vector", sVe, t_mv)
                    em("vector", lambda e: e.tensor_scalar_add(teps[:], mv[:, 1:2],
                                                               LN_EPS))
                    t2 = winc("vector", sVe)
                    wait("scalar", sVe, t2)
                    em("scalar", lambda e: e.activation(rt[:, 0:1], teps[:], AF.Sqrt))
                    t_rt2 = winc("scalar", sSc)
                    wait("vector", sSc, t_rt2)
                    em("vector", lambda e: e.reciprocal(s0[:, 0:1], rt[:, 0:1]))
                    t1 = winc("vector", sVe)
                    wait("vector", sVe, t1)
                    em("vector", lambda e: e.tensor_mul(tmpn[:, 0:1], s0[:, 0:1],
                                                        s0[:, 0:1]))
                    t3 = winc("vector", sVe)
                    wait("vector", sVe, t3)
                    em("vector", lambda e: e.tensor_mul(tmpn[:, 0:1], tmpn[:, 0:1],
                                                        teps[:]))
                    t4 = winc("vector", sVe)
                    wait("vector", sVe, t4)
                    em("vector", lambda e: e.tensor_scalar(tmpn[:, 0:1], tmpn[:, 0:1],
                                                           -0.5, 1.5, OP.mult, OP.add))
                    t5 = winc("vector", sVe)
                    wait("vector", sVe, t5)
                    em("vector", lambda e: e.tensor_mul(s0[:, 0:1], s0[:, 0:1],
                                                        tmpn[:, 0:1]))
                    t6 = winc("vector", sVe)
                    wait("vector", sVe, t6)
                    em("vector", lambda e, o=x2h[:, tc4, :]:
                       e.tensor_scalar(o, o, mv[:, 0:1], s0[:, 0:1],
                                       OP.subtract, OP.mult))
                    t_ln = winc("vector", sVe)
                    wait("gpsimd", sVe, t_ln)
                    gdma(sX2h[hf], x2_bf[tc * 128:(tc + 1) * 128, :], x2h[:, tc4, :])
                    gdma(sXf2[hf], x2f_dr[tc * 128:(tc + 1) * 128, :], x2h[:, tc4, :])
                    drip()
                x2h_free = True

        drip(len(cast_chunks))
        assert not cast_chunks

        with nc.psum_tensor("fps", [128, 8, 512], F32) as fps:
            p1T_release = [None, None]
            p2Tr_release = [None] * NPR
            bank_rel = {}
            out_stored = 0
            wait("sync", sTe, sTe.v)
            wait("sync", sFF, sFF.v)
            wait("scalar", sFF, sFF.v)
            wait("tensor", sSc, sSc.v)
            x2fmb_release = [None]
            ln2_done = 0
            for hf in range(2):
                tc0 = hf * NTCH
                wait("sync", sX2h[hf], sX2h[hf].v)
                if hf == 1:
                    wait("sync", sTe, x2fmb_release[0])
                    wait("sync", sVe, sVe.v)
                    wait("sync", sSc, sSc.v)
                    wait("sync", sTe, sTe.v)
                for kc in range(KC):
                    ytr(sTf, x2fmb[:, kc, :], x2_bf[hf * TH:(hf + 1) * TH,
                                                    kc * 128:(kc + 1) * 128])
                t_x2f = sTf.v
                x2fmb_release = [None]
                h_done = {}
                wait("tensor", sTf, t_x2f)
                wait("scalar", sCst, sCst.v)
                for jg in range(16):
                    sl = jg % 2
                    if p1T_release[sl] is not None:
                        wait("sync", sTe, p1T_release[sl])
                    for e_ in range(16):
                        ytr(sTp[sl], p1T[:, sl, e_, :],
                            p1_bf[jg * 512:(jg + 1) * 512, e_ * 128:(e_ + 1) * 128])
                    wait("tensor", sTp[sl], sTp[sl].v)
                    for j4 in range(4):
                        jc = jg * 4 + j4
                        bank = jc % 8
                        if bank_rel.get(bank) is not None:
                            kind, tgt = bank_rel[bank]
                            wait("tensor", ENG_SEM[kind], tgt)
                        for e_ in range(16):
                            em("tensor", lambda e, o=fps[:, bank, :TH],
                               l=p1T[:, sl, e_, j4 * 128:(j4 + 1) * 128],
                               r=x2fmb[:, e_, :], st=(e_ == 0), sp=(e_ == 15):
                               e.matmul(o, l, r, start=st, stop=sp))
                        t = winc("tensor", sTe)
                        if j4 == 3:
                            p1T_release[sl] = t
                        if jg == 15 and j4 == 3:
                            x2fmb_release[0] = t
                        wait("scalar", sTe, t)
                        em("scalar", lambda e, o=hfmb[:, jc, :], p=fps[:, bank, :TH],
                           b=b1s[:, jc:jc + 1]:
                           e.activation(o, p, AF.Gelu, bias=b))
                        h_done[jc] = winc("scalar", sSc)
                        bank_rel[bank] = ("sc", h_done[jc])
                for s_ in (sXf2[0], sXf2[1], sX2h[0], sX2h[1]):
                    wait("gpsimd", s_, s_.v)
                if hf == 1:
                    wait("gpsimd", sVe, ln2_done)
                for tc4 in range(NTCH):
                    tc = tc0 + tc4
                    gdma(sXrl[hf], x2h[:, tc4, :], x2f_dr[tc * 128:(tc + 1) * 128, :])
                wait("tensor", sCst, sCst.v)
                wait("vector", sVe, t_ones)
                wait("vector", sXrl[hf], sXrl[hf].v)
                es_evac = {}
                for es in range(4):
                    for tc4 in range(NTCH):
                        bank = tc4
                        if bank_rel.get(bank) is not None:
                            kind, tgt = bank_rel[bank]
                            wait("tensor", ENG_SEM[kind], tgt)
                        em("tensor", lambda e, o=fps[:, bank, :],
                           r=b2b[0:1, es * 512:(es + 1) * 512]:
                           e.matmul(o, ones[0:1, :], r, start=True, stop=False))
                    for j in range(JC):
                        rsl = j % NPR
                        if es == 0:
                            wait("tensor", sSc, h_done[j])
                        if p2Tr_release[rsl] is not None:
                            wait("scalar", sTe, p2Tr_release[rsl])
                        str_(sP[rsl], p2Tr[:, rsl, :],
                             p2_bf[es * 512:(es + 1) * 512, j * 128:(j + 1) * 128])
                        wait("tensor", sP[rsl], sP[rsl].v)
                        for tc4 in range(NTCH):
                            em("tensor", lambda e, o=fps[:, tc4, :],
                               l=hfmb[:, j, tc4 * 128:(tc4 + 1) * 128],
                               r=p2Tr[:, rsl, :], sp=(j == JC - 1):
                               e.matmul(o, l, r, start=False, stop=sp))
                        p2Tr_release[rsl] = winc("tensor", sTe)
                    t_es = sTe.v
                    for tc4 in range(NTCH):
                        wait("vector", sTe, t_es)
                        em("vector", lambda e,
                           o=x2h[:, tc4, es * 512:(es + 1) * 512],
                           p=fps[:, tc4, :]: e.tensor_add(o, o, p))
                        es_evac[(es, tc4)] = winc("vector", sVe)
                        bank_rel[tc4] = ("ve", es_evac[(es, tc4)])
                for tc4 in range(NTCH):
                    tc = tc0 + tc4
                    wait("vector", sVe, es_evac[(3, tc4)])
                    for i in range(4):
                        em("vector", lambda e, o=st6[:, 6 * i:6 * (i + 1)],
                           s=x2h[:, tc4, 512 * i:512 * (i + 1)]:
                           e.bn_stats(o, s))
                    t_st = winc("vector", sVe)
                    wait("vector", sVe, t_st)
                    em("vector", lambda e: e.bn_aggr(mv[:], st6[:]))
                    t_mv = winc("vector", sVe)
                    wait("vector", sVe, t_mv)
                    em("vector", lambda e: e.tensor_scalar_add(teps[:], mv[:, 1:2],
                                                               LN_EPS))
                    t2 = winc("vector", sVe)
                    wait("scalar", sVe, t2)
                    em("scalar", lambda e: e.activation(rt[:, 0:1], teps[:], AF.Sqrt))
                    t_rt2 = winc("scalar", sSc)
                    wait("vector", sSc, t_rt2)
                    em("vector", lambda e: e.reciprocal(s0[:, 0:1], rt[:, 0:1]))
                    t1 = winc("vector", sVe)
                    wait("vector", sVe, t1)
                    em("vector", lambda e: e.tensor_mul(tmpn[:, 0:1], s0[:, 0:1],
                                                        s0[:, 0:1]))
                    t3 = winc("vector", sVe)
                    wait("vector", sVe, t3)
                    em("vector", lambda e: e.tensor_mul(tmpn[:, 0:1], tmpn[:, 0:1],
                                                        teps[:]))
                    t4 = winc("vector", sVe)
                    wait("vector", sVe, t4)
                    em("vector", lambda e: e.tensor_scalar(tmpn[:, 0:1], tmpn[:, 0:1],
                                                           -0.5, 1.5, OP.mult, OP.add))
                    t5 = winc("vector", sVe)
                    wait("vector", sVe, t5)
                    em("vector", lambda e: e.tensor_mul(s0[:, 0:1], s0[:, 0:1],
                                                        tmpn[:, 0:1]))
                    t6 = winc("vector", sVe)
                    wait("vector", sVe, t6)
                    if out_stored:
                        wait("vector", sSt, sSt.v)
                    em("vector", lambda e, s=x2h[:, tc4, :]:
                       e.tensor_scalar(outt[:], s, mv[:, 0:1], s0[:, 0:1],
                                       OP.subtract, OP.mult))
                    t_o = winc("vector", sVe)
                    ln2_done = t_o
                    wait("scalar", sVe, t_o)
                    sdma(sSt, out[tc * 128:(tc + 1) * 128, :], outt[:])
                    out_stored += 1

        if taps:
            tapable = {"x_bf": (x_bf, [T, D], BF16), "u_dr": (u_dr, [T, D], F32),
                       "kv_out": (kv_out, [128, D], F32), "nu_bf": (nu_bf, [T, D], BF16),
                       "x2_bf": (x2_bf, [T, D], BF16), "x2f_dr": (x2f_dr, [T, D], F32),
                       "p1_bf": (p1_bf, [HID, D], BF16), "p2_bf": (p2_bf, [D, HID], BF16),
                       "sb_hfmb": (hfmb[:, :, :], [128, JC * TH], BF16),
                       "sb_x2fmb": (x2fmb[:, :, :], [128, KC * TH], BF16),
                       "sb_x2h": (x2h[:, :, :], [128, NTCH * D], F32),
                       "sb_p1T": (p1T[:, :, :, :], [128, 2 * 16 * 512], BF16)}
            sTap = mksem("sTap")
            for s in ([sXbf, sCst, sFF, sKv1, sKv2, sNu, sXr, sSt, sTx, sTn, sTf]
                      + sW + sUb + sUr + sX2h + sXf2 + sXrl + sTp + sP):
                wait("sync", s, s.v)
            for name in taps:
                ap, shp, dt = tapable[name]
                if name.startswith("sb_"):
                    shp = [128, shp[1]] if len(shp) == 2 else shp
                th = nc.dram_tensor("tap_" + name, shp, dt, kind="ExternalOutput")
                t = sTap.inc(16)
                em("sync", lambda e, d=th.ap(), s=ap: e.dma_start(out=d, in_=s).then_inc(sTap.h, 16))
            wait("sync", sTap, sTap.v)

        wait("sync", sSt, sSt.v)
        for s in ([sXbf, sCst, sFF, sKv1, sKv2, sNu, sXr, sTx, sTn, sTf]
                  + sW + sUb + sUr + sX2h + sXf2 + sXrl + sTp + sP):
            wait("sync", s, s.v)

        with nc.Block() as block:
            @block.tensor
            def _(e):
                for f in prog["tensor"]:
                    f(e)

            @block.scalar
            def _(e):
                for f in prog["scalar"]:
                    f(e)

            @block.vector
            def _(e):
                for f in prog["vector"]:
                    f(e)

            @block.sync
            def _(e):
                for f in prog["sync"]:
                    f(e)

            @block.gpsimd
            def _(e):
                for f in prog["gpsimd"]:
                    f(e)

    return nc


_NC_CACHE = {}


def kernel(L0_attn_x, L0_attn_Wq, L0_attn_Wk, L0_attn_Wv, L0_attn_Wu,
           L0_attn_Wo, L0_ffn_proj1, L0_ffn_proj1_bias, L0_ffn_proj2,
           L0_ffn_proj2_bias, L0_ffn_gate_w):
    T = 1024
    x = np.ascontiguousarray(np.asarray(L0_attn_x, dtype=np.float32))
    B, L, _ = x.shape
    per = {
        "wq": np.ascontiguousarray(np.asarray(L0_attn_Wq, dtype=np.float32)),
        "wk": np.ascontiguousarray(np.asarray(L0_attn_Wk, dtype=np.float32)),
        "wv": np.ascontiguousarray(np.asarray(L0_attn_Wv, dtype=np.float32)),
        "wu": np.ascontiguousarray(np.asarray(L0_attn_Wu, dtype=np.float32)),
        "wo": np.ascontiguousarray(np.asarray(L0_attn_Wo, dtype=np.float32)),
        "p1": np.ascontiguousarray(np.asarray(L0_ffn_proj1, dtype=np.float32)),
        "b1": np.ascontiguousarray(np.asarray(L0_ffn_proj1_bias, dtype=np.float32)),
        "p2": np.ascontiguousarray(np.asarray(L0_ffn_proj2, dtype=np.float32)),
        "b2": np.ascontiguousarray(np.asarray(L0_ffn_proj2_bias, dtype=np.float32)),
    }
    if T not in _NC_CACHE:
        _NC_CACHE[T] = build_nc(T)
    nc = _NC_CACHE[T]
    in_maps = []
    for c in range(8):
        b = c // 4
        t0 = (c % 4) * T
        m = dict(per)
        m["x"] = np.ascontiguousarray(x[b, t0:t0 + T])
        in_maps.append(m)
    res = run_bass_kernel_spmd(nc, in_maps, core_ids=list(range(8)))
    full = np.empty((B, L, D), dtype=np.float32)
    for c in range(8):
        b = c // 4
        t0 = (c % 4) * T
        full[b, t0:t0 + T] = res.results[c]["out"]
    return full


# version 06
# speedup vs baseline: 1.1122x; 1.0019x over previous
import math
from contextlib import ExitStack

import numpy as np

import concourse.bass as bass
import concourse.mybir as mybir
from concourse.bass_utils import run_bass_kernel_spmd

F32 = mybir.dt.float32
BF16 = mybir.dt.bfloat16
I8 = mybir.dt.int8
AF = mybir.ActivationFunctionType
OP = mybir.AluOpType

D = 2048
KC = D // 128
H = 16
DH = 128
HID = 8192
JC = HID // 128
SCALE = math.sqrt(DH)
LN_EPS = 1e-5
SRMS_EPS = 1e-12
NPR = 8


class Sem:
    def __init__(self, h):
        self.h = h
        self.v = 0

    def inc(self, n):
        self.v += n
        return self.v


def build_nc(T=1024, taps=()):
    NTC = T // 128
    TT = min(512, T)
    NTT = T // TT
    TH = T // 2
    NTCH = TH // 128

    nc = bass.Bass("TRN2", target_bir_lowering=False)

    x_h = nc.dram_tensor("x", [T, D], F32, kind="ExternalInput")
    wq_h = nc.dram_tensor("wq", [D, D], F32, kind="ExternalInput")
    wk_h = nc.dram_tensor("wk", [D, D], F32, kind="ExternalInput")
    wv_h = nc.dram_tensor("wv", [D, D], F32, kind="ExternalInput")
    wu_h = nc.dram_tensor("wu", [D, D], F32, kind="ExternalInput")
    wo_h = nc.dram_tensor("wo", [D, D], F32, kind="ExternalInput")
    p1_h = nc.dram_tensor("p1", [HID, D], F32, kind="ExternalInput")
    b1_h = nc.dram_tensor("b1", [HID], F32, kind="ExternalInput")
    p2_h = nc.dram_tensor("p2", [D, HID], F32, kind="ExternalInput")
    b2_h = nc.dram_tensor("b2", [D], F32, kind="ExternalInput")
    out_h = nc.dram_tensor("out", [T, D], F32, kind="ExternalOutput")

    x_bf = nc.dram_tensor("x_bf", [T, D], BF16).ap()
    nu_bf = nc.dram_tensor("nu_bf", [T, D], BF16).ap()
    x2_bf = nc.dram_tensor("x2_bf", [T, D], BF16).ap()
    x2f_dr = nc.dram_tensor("x2f_dr", [T, D], F32).ap()
    p1_bf = nc.dram_tensor("p1_bf", [HID, D], BF16).ap()
    p2_bf = nc.dram_tensor("p2_bf", [D, HID], BF16).ap()
    u_dr = nc.dram_tensor("u_dr", [T, D], F32).ap()
    kv_in = nc.dram_tensor("kv_in", [128, D], F32).ap()
    kv_out = nc.dram_tensor("kv_out", [128, D], F32).ap()

    x = x_h.ap()
    out = out_h.ap()
    w_ap = {"q": wq_h.ap(), "k": wk_h.ap(), "v": wv_h.ap(),
            "u": wu_h.ap(), "o": wo_h.ap()}

    R1 = 10240
    R2 = R1 + 65536
    R3 = R2 + 65536
    R4 = R3 + 32768
    R5 = R4 + 8192
    ARENA = R5 + 8192
    nc.alloc_sbuf_tensor("arena", [128, ARENA], I8)
    base = nc.lookup_mloc("arena").addr

    def sbuf(name, shape, dtype, off):
        return nc.alloc_sbuf_tensor_at(name, shape, dtype, offset=base + off)

    off = 0
    kvb = sbuf("kvb", [128, D], BF16, off); off += 4096
    b2b = sbuf("b2b", [1, D], BF16, off); off += 4096
    b1s = sbuf("b1s", [128, JC], F32, off); off += 256
    ones = sbuf("ones", [1, 128], BF16, off); off += 256
    ssq = sbuf("ssq", [128, H], F32, off); off += 64
    st6 = sbuf("st6", [128, 24], F32, off); off += 128
    mv = sbuf("mv", [128, 2], F32, off); off += 64
    rt = sbuf("rt", [128, H], F32, off); off += 64
    s0 = sbuf("s0", [128, H], F32, off); off += 64
    tmpn = sbuf("tmpn", [128, H], F32, off); off += 64
    teps = sbuf("teps", [128, 1], F32, off); off += 64
    st6b = sbuf("st6b", [128, 24], F32, off); off += 128
    mvb = sbuf("mvb", [128, 2], F32, off); off += 64
    rtb = sbuf("rtb", [128, 1], F32, off); off += 64
    s0b = sbuf("s0b", [128, 1], F32, off); off += 64
    tmpb = sbuf("tmpb", [128, 1], F32, off); off += 64
    tepsb = sbuf("tepsb", [128, 1], F32, off); off += 64
    assert off <= R1

    xfmb = sbuf("xfmb", [128, KC, T], BF16, R1)
    qfm = sbuf("qfm", [128, KC, T], BF16, R1 + 32768)
    hfmb = sbuf("hfmb", [128, JC, TH], BF16, R1)
    ktm = sbuf("ktm", [128, NTC, D], BF16, R2)
    kvf = sbuf("kvf", [128, D], F32, R2)
    vtm = sbuf("vtm", [128, NTC, D], BF16, R2 + NTC * D * 2)
    x2h = sbuf("x2h", [128, NTCH, D], F32, R2)
    nuT = sbuf("nuT", [128, KC, TH], BF16, R2 + 32768)
    urld = sbuf("urld", [128, 2, D], F32, R2 + 49152)
    p1T = sbuf("p1T", [128, 2, 16, 512], BF16, R2 + 32768)
    p2Tr = sbuf("p2Tr", [128, NPR, 512], BF16, R3 + 16384)
    outt = sbuf("outt", [128, D], F32, R3 + 24576)
    wstr = sbuf("wstr", [128, 2, 16, 512], BF16, R3)
    x2fmb = sbuf("x2fmb", [128, KC, TH], BF16, R3)
    ubnc = sbuf("ubnc", [128, 4, 512], F32, R4)
    xres = sbuf("xres", [128, D], F32, R4)
    nut = sbuf("nut", [128, 2, D], BF16, R5)

    prog = {k: [] for k in ("tensor", "scalar", "vector", "sync", "gpsimd")}

    def em(eng, fn):
        prog[eng].append(fn)

    def wait(eng, sem, val):
        if val > 0:
            em(eng, lambda e, s=sem.h, v=val: e.wait_ge(s, v))

    def winc(eng, sem, n=1):
        t = sem.inc(n)
        last = prog[eng].pop()
        em(eng, lambda e, f=last, s=sem.h, m=n: f(e).then_inc(s, m))
        return t

    with ExitStack() as stack:
        def mksem(name):
            return Sem(stack.enter_context(nc.semaphore(name)))

        sTe, sSc, sVe = mksem("sTe"), mksem("sSc"), mksem("sVe")
        sCc = mksem("sCc")
        ENG_SEM = {"sc": sSc, "ve": sVe, "te": sTe}
        sXbf = mksem("sXbf")
        sCst = mksem("sCst")
        sFF = mksem("sFF")
        sKv1, sKv2 = mksem("sKv1"), mksem("sKv2")
        sW = [mksem("sW0"), mksem("sW1")]
        sUb = [mksem(f"sUb{i}") for i in range(4)]
        sUr = [mksem("sUr0"), mksem("sUr1")]
        sNu = mksem("sNu")
        sX2h = [mksem("sX2a"), mksem("sX2b")]
        sXf2 = [mksem("sXfa"), mksem("sXfb")]
        sXrl = [mksem("sXra"), mksem("sXrb")]
        sXr = mksem("sXr")
        sSt = mksem("sSt")
        sTx = mksem("sTx")
        sTn = mksem("sTn")
        sTf = mksem("sTf")
        sTp = [mksem("sTp0"), mksem("sTp1")]
        sP = [mksem(f"sP{i}") for i in range(NPR)]

        def gdma(sem, dst, src, **kw):
            t = sem.inc(16)
            em("gpsimd", lambda e, d=dst, s=src, k=kw:
               e.dma_start(out=d, in_=s, **k).then_inc(sem.h, 16))
            return t

        def ytr(sem, dst, src):
            t = sem.inc(16)
            em("sync", lambda e, d=dst, s=src:
               e.dma_start_transpose(d, s).then_inc(sem.h, 16))
            return t

        def str_(sem, dst, src):
            t = sem.inc(16)
            em("scalar", lambda e, d=dst, s=src:
               e.dma_start_transpose(d, s).then_inc(sem.h, 16))
            return t

        def sdma(sem, dst, src, **kw):
            t = sem.inc(16)
            em("scalar", lambda e, d=dst, s=src, k=kw:
               e.dma_start(out=d, in_=s, **k).then_inc(sem.h, 16))
            return t

        gdma(sXbf, x_bf, x)
        gdma(sCst, b2b[0:1, :], b2_h.ap()[None, :])
        em("vector", lambda e: e.memset(ones[:], 1.0))
        t_ones = winc("vector", sVe)

        wait("sync", sXbf, sXbf.v)
        for kc in range(KC):
            ytr(sTx, xfmb[:, kc, :], x_bf[:, kc * 128:(kc + 1) * 128])

        cast_chunks = []
        for i in range(32):
            cast_chunks.append((p1_bf[i * 256:(i + 1) * 256, :],
                                p1_h.ap()[i * 256:(i + 1) * 256, :]))
        for i in range(32):
            cast_chunks.append((p2_bf[i * 64:(i + 1) * 64, :],
                                p2_h.ap()[i * 64:(i + 1) * 64, :]))

        def drip(n=1):
            for _ in range(n):
                if cast_chunks:
                    d, s = cast_chunks.pop(0)
                    gdma(sFF, d, s)

        with (
            nc.psum_tensor("kvps", [128, D], F32) as kvps,
            nc.psum_tensor("pj0", [128, 512], F32) as pj0,
            nc.psum_tensor("pj1", [128, 512], F32) as pj1,
        ):
            pj = [pj0, pj1]

            state = {"oscnt": 0}
            slot_release = [None, None]
            load_done = {}

            def load_wtiles(proj, wap, os, slot):
                if slot_release[slot] is not None:
                    wait("gpsimd", sTe, slot_release[slot])
                for k in range(16):
                    gdma(sW[slot], wstr[:, slot, k, :],
                         wap[k * 128:(k + 1) * 128, os * 512:(os + 1) * 512])
                    if proj in ("k", "v") and k in (3, 7, 11, 15):
                        drip()
                    elif proj == "q" and k in (5, 11, 15):
                        drip()
                load_done[(proj, os)] = (slot, sW[slot].v)

            def wait_wtiles(proj, os):
                slot, tot = load_done[(proj, os)]
                wait("tensor", sW[slot], tot)
                return slot

            pjcnt = [0]
            pj_release = {}

            def pj_guard(idx):
                if idx >= 2 and pj_release.get(idx - 2) is not None:
                    kind, tgt = pj_release[idx - 2]
                    wait("tensor", ENG_SEM[kind], tgt)

            def stream_proj_tm(proj, wap, evac):
                for os in range(4):
                    slot = state["oscnt"] % 2
                    load_wtiles(proj, wap, os, slot)
                    wait_wtiles(proj, os)
                    for tc in range(NTC):
                        idx = pjcnt[0]
                        ps = pj[idx % 2]
                        pj_guard(idx)
                        for k in range(16):
                            em("tensor", lambda e, p=ps,
                               l=xfmb[:, k, tc * 128:(tc + 1) * 128],
                               r=wstr[:, slot, k, :], st=(k == 0), sp=(k == 15):
                               e.matmul(p[:], l, r, start=st, stop=sp))
                        t = winc("tensor", sTe)
                        if tc == NTC - 1:
                            slot_release[slot] = t
                        pj_release[idx] = evac(os, tc, ps, t)
                        pjcnt[0] += 1
                    state["oscnt"] += 1

            def evac_k(os, tc, ps, t):
                wait("scalar", sTe, t)
                em("scalar", lambda e, o=ktm[:, tc, os * 512:(os + 1) * 512], p=ps:
                   e.activation(o, p[:], AF.Relu, scale=1.0 / SCALE))
                return ("sc", winc("scalar", sSc))

            def evac_v(os, tc, ps, t):
                wait("scalar", sTe, t)
                em("scalar", lambda e, o=vtm[:, tc, os * 512:(os + 1) * 512], p=ps:
                   e.copy(o, p[:]))
                return ("sc", winc("scalar", sSc))

            ucnt = [0]

            def evac_u(os, tc, ps, t):
                ui = ucnt[0] % 4
                wait("scalar", sTe, t)
                if ucnt[0] >= 4:
                    wait("scalar", sUb[ui], sUb[ui].v)
                em("scalar", lambda e, o=ubnc[:, ui, :], p=ps:
                   e.activation(o, p[:], AF.Silu))
                tsc = winc("scalar", sSc)
                wait("scalar", sSc, tsc)
                sdma(sUb[ui], u_dr[tc * 128:(tc + 1) * 128, os * 512:(os + 1) * 512],
                     ubnc[:, ui, :])
                ucnt[0] += 1
                return ("sc", tsc)

            wait("tensor", sTx, sTx.v)

            stream_proj_tm("k", w_ap["k"], evac_k)
            stream_proj_tm("v", w_ap["v"], evac_v)

            wait("tensor", sSc, sSc.v)
            for h in range(H):
                for tcc in range(NTC):
                    em("tensor", lambda e, o=kvps[:, h * 128:(h + 1) * 128],
                       l=ktm[:, tcc, h * 128:(h + 1) * 128],
                       r=vtm[:, tcc, h * 128:(h + 1) * 128],
                       st=(tcc == 0), sp=(tcc == NTC - 1):
                       e.matmul(o, l, r, start=st, stop=sp))
            t_kvmm = winc("tensor", sTe)
            wait("scalar", sTe, t_kvmm)
            em("scalar", lambda e: e.copy(kvf[:], kvps[:]))
            t_kvf = winc("scalar", sSc)
            wait("gpsimd", sSc, t_kvf)
            gdma(sKv1, kv_in, kvf[:])
            wait("gpsimd", sKv1, sKv1.v)
            em("gpsimd", lambda e: e.collective_compute(
                "AllReduce", OP.add,
                replica_groups=[[0, 1, 2, 3], [4, 5, 6, 7]],
                ins=[kv_in], outs=[kv_out]))
            t_cc = winc("gpsimd", sCc)

            stream_proj_tm("u", w_ap["u"], evac_u)

            for os in range(4):
                slot = state["oscnt"] % 2
                load_wtiles("q", w_ap["q"], os, slot)
                wait_wtiles("q", os)
                for oc in range(4):
                    for tt in range(NTT):
                        idx = pjcnt[0]
                        ps = pj[idx % 2]
                        pj_guard(idx)
                        for k in range(16):
                            em("tensor", lambda e, p=ps,
                               l=wstr[:, slot, k, oc * 128:(oc + 1) * 128],
                               r=xfmb[:, k, tt * TT:(tt + 1) * TT],
                               st=(k == 0), sp=(k == 15):
                               e.matmul(p[:, :TT], l, r, start=st, stop=sp))
                        t = winc("tensor", sTe)
                        if oc == 3 and tt == NTT - 1:
                            slot_release[slot] = t
                        ocg = os * 4 + oc
                        wait("scalar", sTe, t)
                        em("scalar", lambda e,
                           o=qfm[:, ocg, tt * TT:(tt + 1) * TT], p=ps:
                           e.activation(o, p[:, :TT], AF.Relu, scale=1.0 / SCALE))
                        pj_release[idx] = ("sc", winc("scalar", sSc))
                        pjcnt[0] += 1
                state["oscnt"] += 1
            wait("gpsimd", sCc, t_cc)
            gdma(sKv2, kvb[:], kv_out)
            for j in range(JC):
                gdma(sCst, b1s[:, j:j + 1], b1_h.ap()[j * 128:(j + 1) * 128, None])
            for back in (2, 1):
                r = pj_release.get(pjcnt[0] - back)
                if r is not None:
                    kind, tgt = r
                    wait("tensor", ENG_SEM[kind], tgt)

        t_q_evacs = sSc.v

        with (
            nc.psum_tensor("aops", [128, D], F32) as aops,
            nc.psum_tensor("yps0", [128, 512], F32) as yps0,
            nc.psum_tensor("yps1", [128, 512], F32) as yps1,
        ):
            yps = [yps0, yps1]
            wait("tensor", sVe, sVe.v)
            wait("tensor", sSc, sSc.v)
            wait("gpsimd", sTe, sTe.v)
            for i in range(4):
                wait("gpsimd", sUb[i], sUb[i].v)
            wait("tensor", sKv2, sKv2.v)
            wait("tensor", sSc, t_q_evacs)
            aops_release = None
            urld_release = [None, None]
            prev_tnu = 0
            ycnt = 0
            y_release = {}
            xres_release = None
            wo_last_mm = 0
            for hf in range(2):
                tc0 = hf * NTCH
                for tc4 in range(NTCH):
                    tc = tc0 + tc4
                    sl = tc % 2
                    if urld_release[sl] is not None:
                        wait("gpsimd", sVe, urld_release[sl])
                    gdma(sUr[sl], urld[:, sl, :], u_dr[tc * 128:(tc + 1) * 128, :])
                    t_u = sUr[sl].v
                    drip()
                    if aops_release is not None:
                        wait("tensor", sVe, aops_release)
                    for h in range(H):
                        em("tensor", lambda e, o=aops[:, h * 128:(h + 1) * 128],
                           l=qfm[:, h, tc * 128:(tc + 1) * 128],
                           r=kvb[:, h * 128:(h + 1) * 128]:
                           e.matmul(o, l, r, start=True, stop=True))
                    t_ao = winc("tensor", sTe)
                    wait("scalar", sTe, t_ao)
                    if prev_tnu:
                        wait("scalar", sVe, prev_tnu)
                    if tc >= 2:
                        wait("scalar", sNu, sNu.v)
                    for h in range(H):
                        em("scalar", lambda e, o=nut[:, sl, h * 128:(h + 1) * 128],
                           p=aops[:, h * 128:(h + 1) * 128], a=ssq[:, h:h + 1]:
                           e.activation(o, p, AF.Square, accum_out=a))
                    t_sq = winc("scalar", sSc)
                    wait("scalar", sSc, t_sq)
                    em("scalar", lambda e: e.activation(rt[:], ssq[:], AF.Sqrt,
                                                        scale=1.0 / DH))
                    t_rt = winc("scalar", sSc)
                    wait("vector", sSc, t_rt)
                    em("vector", lambda e: e.tensor_scalar_max(rt[:], rt[:], SRMS_EPS))
                    t1 = winc("vector", sVe)
                    wait("vector", sVe, t1)
                    em("vector", lambda e: e.reciprocal(s0[:], rt[:]))
                    t2 = winc("vector", sVe)
                    wait("vector", sVe, t2)
                    em("vector", lambda e: e.tensor_mul(tmpn[:], s0[:], s0[:]))
                    t3 = winc("vector", sVe)
                    wait("vector", sVe, t3)
                    em("vector", lambda e: e.tensor_mul(tmpn[:], tmpn[:], ssq[:]))
                    t4 = winc("vector", sVe)
                    wait("vector", sVe, t4)
                    em("vector", lambda e: e.tensor_scalar(tmpn[:], tmpn[:],
                                                           -0.5 / DH, 1.5,
                                                           OP.mult, OP.add))
                    t5 = winc("vector", sVe)
                    wait("vector", sVe, t5)
                    em("vector", lambda e: e.tensor_mul(s0[:], s0[:], tmpn[:]))
                    t6 = winc("vector", sVe)
                    wait("vector", sVe, t6)
                    wait("vector", sUr[sl], t_u)
                    for h in range(H):
                        em("vector", lambda e, o=nut[:, sl, h * 128:(h + 1) * 128],
                           p=aops[:, h * 128:(h + 1) * 128], s=s0[:, h:h + 1],
                           u=urld[:, sl, h * 128:(h + 1) * 128]:
                           e.scalar_tensor_tensor(o, p, s, u, OP.mult, OP.mult))
                    t_nu = winc("vector", sVe)
                    prev_tnu = t_nu
                    aops_release = t_nu
                    urld_release[sl] = t_nu
                    wait("gpsimd", sVe, t_nu)
                    gdma(sNu, nu_bf[tc * 128:(tc + 1) * 128, :], nut[:, sl, :])
                wait("sync", sNu, sNu.v)
                if hf == 0:
                    wait("sync", sTe, sTe.v)
                else:
                    wait("sync", sTe, wo_last_mm)
                for kc in range(KC):
                    ytr(sTn, nuT[:, kc, :],
                        nu_bf[hf * TH:(hf + 1) * TH, kc * 128:(kc + 1) * 128])
                t_nuT = sTn.v
                wait("tensor", sTn, t_nuT)
                for os in range(4):
                    slot = state["oscnt"] % 2
                    load_wtiles(("o", hf), w_ap["o"], os, slot)
                    wait_wtiles(("o", hf), os)
                    for tc4 in range(NTCH):
                        tc = tc0 + tc4
                        ps = yps[ycnt % 2]
                        if ycnt >= 2 and y_release.get(ycnt - 2) is not None:
                            wait("tensor", sSc, y_release[ycnt - 2])
                        for k in range(16):
                            em("tensor", lambda e, p=ps,
                               l=nuT[:, k, tc4 * 128:(tc4 + 1) * 128],
                               r=wstr[:, slot, k, :], st=(k == 0), sp=(k == 15):
                               e.matmul(p[:], l, r, start=st, stop=sp))
                        t = winc("tensor", sTe)
                        wo_last_mm = t
                        if tc4 == NTCH - 1:
                            slot_release[slot] = t
                        wait("scalar", sTe, t)
                        if hf == 1 and os == 0 and tc4 == 0:
                            wait("scalar", sX2h[0], sX2h[0].v)
                            wait("scalar", sXf2[0], sXf2[0].v)
                        em("scalar", lambda e,
                           o=x2h[:, tc4, os * 512:(os + 1) * 512],
                           p=ps: e.copy(o, p[:]))
                        y_release[ycnt] = winc("scalar", sSc)
                        ycnt += 1
                    state["oscnt"] += 1
                t_evac_half = sSc.v
                wait("vector", sSc, t_evac_half)
                for tc4 in range(NTCH):
                    tc = tc0 + tc4
                    if xres_release is not None:
                        wait("gpsimd", sVe, xres_release)
                    gdma(sXr, xres[:], x[tc * 128:(tc + 1) * 128, :])
                    drip()
                    wait("vector", sXr, sXr.v)
                    em("vector", lambda e, o=x2h[:, tc4, :]:
                       e.tensor_add(o, o, xres[:]))
                    t_add = winc("vector", sVe)
                    xres_release = t_add
                    wait("vector", sVe, t_add)
                    for i in range(4):
                        em("vector", lambda e, o=st6b[:, 6 * i:6 * (i + 1)],
                           s=x2h[:, tc4, 512 * i:512 * (i + 1)]:
                           e.bn_stats(o, s))
                    t_st = winc("vector", sVe)
                    wait("vector", sVe, t_st)
                    em("vector", lambda e: e.bn_aggr(mvb[:], st6b[:]))
                    t_mv = winc("vector", sVe)
                    wait("vector", sVe, t_mv)
                    em("vector", lambda e: e.tensor_scalar_add(tepsb[:], mvb[:, 1:2],
                                                               LN_EPS))
                    t2 = winc("vector", sVe)
                    wait("scalar", sVe, t2)
                    em("scalar", lambda e: e.activation(rtb[:], tepsb[:], AF.Sqrt))
                    t_rt2 = winc("scalar", sSc)
                    wait("vector", sSc, t_rt2)
                    em("vector", lambda e: e.reciprocal(s0b[:], rtb[:]))
                    t1 = winc("vector", sVe)
                    wait("vector", sVe, t1)
                    em("vector", lambda e: e.tensor_mul(tmpb[:], s0b[:],
                                                        s0b[:]))
                    t3 = winc("vector", sVe)
                    wait("vector", sVe, t3)
                    em("vector", lambda e: e.tensor_mul(tmpb[:], tmpb[:],
                                                        tepsb[:]))
                    t4 = winc("vector", sVe)
                    wait("vector", sVe, t4)
                    em("vector", lambda e: e.tensor_scalar(tmpb[:], tmpb[:],
                                                           -0.5, 1.5, OP.mult, OP.add))
                    t5 = winc("vector", sVe)
                    wait("vector", sVe, t5)
                    em("vector", lambda e: e.tensor_mul(s0b[:], s0b[:],
                                                        tmpb[:]))
                    t6 = winc("vector", sVe)
                    wait("vector", sVe, t6)
                    em("vector", lambda e, o=x2h[:, tc4, :]:
                       e.tensor_scalar(o, o, mvb[:, 0:1], s0b[:],
                                       OP.subtract, OP.mult))
                    t_ln = winc("vector", sVe)
                    wait("gpsimd", sVe, t_ln)
                    gdma(sX2h[hf], x2_bf[tc * 128:(tc + 1) * 128, :], x2h[:, tc4, :])
                    gdma(sXf2[hf], x2f_dr[tc * 128:(tc + 1) * 128, :], x2h[:, tc4, :])
                    drip()

        drip(len(cast_chunks))
        assert not cast_chunks

        with nc.psum_tensor("fps", [128, 8, 512], F32) as fps:
            p1T_release = [None, None]
            p2Tr_release = [None] * NPR
            bank_rel = {}
            out_stored = 0
            wait("sync", sTe, sTe.v)
            wait("sync", sFF, sFF.v)
            wait("scalar", sFF, sFF.v)
            wait("tensor", sSc, sSc.v)
            wait("tensor", sVe, sVe.v)
            x2fmb_release = [None]
            ln2_done = 0
            for hf in range(2):
                tc0 = hf * NTCH
                wait("sync", sX2h[hf], sX2h[hf].v)
                if hf == 1:
                    wait("sync", sTe, x2fmb_release[0])
                    wait("sync", sVe, sVe.v)
                    wait("sync", sSc, sSc.v)
                    wait("sync", sTe, sTe.v)
                for kc in range(KC):
                    ytr(sTf, x2fmb[:, kc, :], x2_bf[hf * TH:(hf + 1) * TH,
                                                    kc * 128:(kc + 1) * 128])
                t_x2f = sTf.v
                x2fmb_release = [None]
                h_done = {}
                wait("tensor", sTf, t_x2f)
                wait("scalar", sCst, sCst.v)
                for jg in range(16):
                    sl = jg % 2
                    if p1T_release[sl] is not None:
                        wait("sync", sTe, p1T_release[sl])
                    for e_ in range(16):
                        ytr(sTp[sl], p1T[:, sl, e_, :],
                            p1_bf[jg * 512:(jg + 1) * 512, e_ * 128:(e_ + 1) * 128])
                    wait("tensor", sTp[sl], sTp[sl].v)
                    for j4 in range(4):
                        jc = jg * 4 + j4
                        bank = jc % 8
                        if bank_rel.get(bank) is not None:
                            kind, tgt = bank_rel[bank]
                            wait("tensor", ENG_SEM[kind], tgt)
                        for e_ in range(16):
                            em("tensor", lambda e, o=fps[:, bank, :TH],
                               l=p1T[:, sl, e_, j4 * 128:(j4 + 1) * 128],
                               r=x2fmb[:, e_, :], st=(e_ == 0), sp=(e_ == 15):
                               e.matmul(o, l, r, start=st, stop=sp))
                        t = winc("tensor", sTe)
                        if j4 == 3:
                            p1T_release[sl] = t
                        if jg == 15 and j4 == 3:
                            x2fmb_release[0] = t
                        wait("scalar", sTe, t)
                        em("scalar", lambda e, o=hfmb[:, jc, :], p=fps[:, bank, :TH],
                           b=b1s[:, jc:jc + 1]:
                           e.activation(o, p, AF.Gelu, bias=b))
                        h_done[jc] = winc("scalar", sSc)
                        bank_rel[bank] = ("sc", h_done[jc])
                for s_ in (sXf2[0], sXf2[1], sX2h[0], sX2h[1]):
                    wait("gpsimd", s_, s_.v)
                if hf == 1:
                    wait("gpsimd", sVe, ln2_done)
                for tc4 in range(NTCH):
                    tc = tc0 + tc4
                    gdma(sXrl[hf], x2h[:, tc4, :], x2f_dr[tc * 128:(tc + 1) * 128, :])
                wait("tensor", sCst, sCst.v)
                wait("vector", sVe, t_ones)
                wait("vector", sXrl[hf], sXrl[hf].v)
                es_evac = {}
                for es in range(4):
                    for tc4 in range(NTCH):
                        bank = tc4
                        if bank_rel.get(bank) is not None:
                            kind, tgt = bank_rel[bank]
                            wait("tensor", ENG_SEM[kind], tgt)
                        em("tensor", lambda e, o=fps[:, bank, :],
                           r=b2b[0:1, es * 512:(es + 1) * 512]:
                           e.matmul(o, ones[0:1, :], r, start=True, stop=False))
                    for j in range(JC):
                        rsl = j % NPR
                        if es == 0:
                            wait("tensor", sSc, h_done[j])
                        if p2Tr_release[rsl] is not None:
                            wait("scalar", sTe, p2Tr_release[rsl])
                        str_(sP[rsl], p2Tr[:, rsl, :],
                             p2_bf[es * 512:(es + 1) * 512, j * 128:(j + 1) * 128])
                        wait("tensor", sP[rsl], sP[rsl].v)
                        for tc4 in range(NTCH):
                            em("tensor", lambda e, o=fps[:, tc4, :],
                               l=hfmb[:, j, tc4 * 128:(tc4 + 1) * 128],
                               r=p2Tr[:, rsl, :], sp=(j == JC - 1):
                               e.matmul(o, l, r, start=False, stop=sp))
                        p2Tr_release[rsl] = winc("tensor", sTe)
                    t_es = sTe.v
                    for tc4 in range(NTCH):
                        wait("vector", sTe, t_es)
                        em("vector", lambda e,
                           o=x2h[:, tc4, es * 512:(es + 1) * 512],
                           p=fps[:, tc4, :]: e.tensor_add(o, o, p))
                        es_evac[(es, tc4)] = winc("vector", sVe)
                        bank_rel[tc4] = ("ve", es_evac[(es, tc4)])
                for tc4 in range(NTCH):
                    tc = tc0 + tc4
                    wait("vector", sVe, es_evac[(3, tc4)])
                    for i in range(4):
                        em("vector", lambda e, o=st6b[:, 6 * i:6 * (i + 1)],
                           s=x2h[:, tc4, 512 * i:512 * (i + 1)]:
                           e.bn_stats(o, s))
                    t_st = winc("vector", sVe)
                    wait("vector", sVe, t_st)
                    em("vector", lambda e: e.bn_aggr(mvb[:], st6b[:]))
                    t_mv = winc("vector", sVe)
                    wait("vector", sVe, t_mv)
                    em("vector", lambda e: e.tensor_scalar_add(tepsb[:], mvb[:, 1:2],
                                                               LN_EPS))
                    t2 = winc("vector", sVe)
                    wait("scalar", sVe, t2)
                    em("scalar", lambda e: e.activation(rtb[:], tepsb[:], AF.Sqrt))
                    t_rt2 = winc("scalar", sSc)
                    wait("vector", sSc, t_rt2)
                    em("vector", lambda e: e.reciprocal(s0b[:], rtb[:]))
                    t1 = winc("vector", sVe)
                    wait("vector", sVe, t1)
                    em("vector", lambda e: e.tensor_mul(tmpb[:], s0b[:],
                                                        s0b[:]))
                    t3 = winc("vector", sVe)
                    wait("vector", sVe, t3)
                    em("vector", lambda e: e.tensor_mul(tmpb[:], tmpb[:],
                                                        tepsb[:]))
                    t4 = winc("vector", sVe)
                    wait("vector", sVe, t4)
                    em("vector", lambda e: e.tensor_scalar(tmpb[:], tmpb[:],
                                                           -0.5, 1.5, OP.mult, OP.add))
                    t5 = winc("vector", sVe)
                    wait("vector", sVe, t5)
                    em("vector", lambda e: e.tensor_mul(s0b[:], s0b[:],
                                                        tmpb[:]))
                    t6 = winc("vector", sVe)
                    wait("vector", sVe, t6)
                    if out_stored:
                        wait("vector", sSt, sSt.v)
                    em("vector", lambda e, s=x2h[:, tc4, :]:
                       e.tensor_scalar(outt[:], s, mvb[:, 0:1], s0b[:],
                                       OP.subtract, OP.mult))
                    t_o = winc("vector", sVe)
                    ln2_done = t_o
                    wait("scalar", sVe, t_o)
                    sdma(sSt, out[tc * 128:(tc + 1) * 128, :], outt[:])
                    out_stored += 1

        if taps:
            tapable = {"x_bf": (x_bf, [T, D], BF16), "u_dr": (u_dr, [T, D], F32),
                       "kv_out": (kv_out, [128, D], F32), "nu_bf": (nu_bf, [T, D], BF16),
                       "x2_bf": (x2_bf, [T, D], BF16), "x2f_dr": (x2f_dr, [T, D], F32),
                       "p1_bf": (p1_bf, [HID, D], BF16), "p2_bf": (p2_bf, [D, HID], BF16),
                       "sb_hfmb": (hfmb[:, :, :], [128, JC * TH], BF16),
                       "sb_x2fmb": (x2fmb[:, :, :], [128, KC * TH], BF16),
                       "sb_x2h": (x2h[:, :, :], [128, NTCH * D], F32),
                       "sb_p1T": (p1T[:, :, :, :], [128, 2 * 16 * 512], BF16)}
            sTap = mksem("sTap")
            for s in ([sXbf, sCst, sFF, sKv1, sKv2, sNu, sXr, sSt, sTx, sTn, sTf]
                      + sW + sUb + sUr + sX2h + sXf2 + sXrl + sTp + sP):
                wait("sync", s, s.v)
            for name in taps:
                ap, shp, dt = tapable[name]
                if name.startswith("sb_"):
                    shp = [128, shp[1]] if len(shp) == 2 else shp
                th = nc.dram_tensor("tap_" + name, shp, dt, kind="ExternalOutput")
                t = sTap.inc(16)
                em("sync", lambda e, d=th.ap(), s=ap: e.dma_start(out=d, in_=s).then_inc(sTap.h, 16))
            wait("sync", sTap, sTap.v)

        wait("sync", sSt, sSt.v)
        for s in ([sXbf, sCst, sFF, sKv1, sKv2, sNu, sXr, sTx, sTn, sTf]
                  + sW + sUb + sUr + sX2h + sXf2 + sXrl + sTp + sP):
            wait("sync", s, s.v)

        with nc.Block() as block:
            @block.tensor
            def _(e):
                for f in prog["tensor"]:
                    f(e)

            @block.scalar
            def _(e):
                for f in prog["scalar"]:
                    f(e)

            @block.vector
            def _(e):
                for f in prog["vector"]:
                    f(e)

            @block.sync
            def _(e):
                for f in prog["sync"]:
                    f(e)

            @block.gpsimd
            def _(e):
                for f in prog["gpsimd"]:
                    f(e)

    return nc


_NC_CACHE = {}


def kernel(L0_attn_x, L0_attn_Wq, L0_attn_Wk, L0_attn_Wv, L0_attn_Wu,
           L0_attn_Wo, L0_ffn_proj1, L0_ffn_proj1_bias, L0_ffn_proj2,
           L0_ffn_proj2_bias, L0_ffn_gate_w):
    T = 1024
    x = np.ascontiguousarray(np.asarray(L0_attn_x, dtype=np.float32))
    B, L, _ = x.shape
    per = {
        "wq": np.ascontiguousarray(np.asarray(L0_attn_Wq, dtype=np.float32)),
        "wk": np.ascontiguousarray(np.asarray(L0_attn_Wk, dtype=np.float32)),
        "wv": np.ascontiguousarray(np.asarray(L0_attn_Wv, dtype=np.float32)),
        "wu": np.ascontiguousarray(np.asarray(L0_attn_Wu, dtype=np.float32)),
        "wo": np.ascontiguousarray(np.asarray(L0_attn_Wo, dtype=np.float32)),
        "p1": np.ascontiguousarray(np.asarray(L0_ffn_proj1, dtype=np.float32)),
        "b1": np.ascontiguousarray(np.asarray(L0_ffn_proj1_bias, dtype=np.float32)),
        "p2": np.ascontiguousarray(np.asarray(L0_ffn_proj2, dtype=np.float32)),
        "b2": np.ascontiguousarray(np.asarray(L0_ffn_proj2_bias, dtype=np.float32)),
    }
    if T not in _NC_CACHE:
        _NC_CACHE[T] = build_nc(T)
    nc = _NC_CACHE[T]
    in_maps = []
    for c in range(8):
        b = c // 4
        t0 = (c % 4) * T
        m = dict(per)
        m["x"] = np.ascontiguousarray(x[b, t0:t0 + T])
        in_maps.append(m)
    res = run_bass_kernel_spmd(nc, in_maps, core_ids=list(range(8)))
    full = np.empty((B, L, D), dtype=np.float32)
    for c in range(8):
        b = c // 4
        t0 = (c % 4) * T
        full[b, t0:t0 + T] = res.results[c]["out"]
    return full


# version 07 (final)
# speedup vs baseline: 1.1692x; 1.0513x over previous
import math
from contextlib import ExitStack

import numpy as np

import concourse.bass as bass
import concourse.mybir as mybir
from concourse.bass_utils import run_bass_kernel_spmd

F32 = mybir.dt.float32
BF16 = mybir.dt.bfloat16
I8 = mybir.dt.int8
AF = mybir.ActivationFunctionType
OP = mybir.AluOpType

D = 2048
KC = D // 128
H = 16
DH = 128
HID = 8192
JC = HID // 128
SCALE = math.sqrt(DH)
LN_EPS = 1e-5
SRMS_EPS = 1e-12
NPR = 8


class Sem:
    def __init__(self, h):
        self.h = h
        self.v = 0

    def inc(self, n):
        self.v += n
        return self.v


def build_nc(T=1024, taps=()):
    NTC = T // 128
    TT = min(512, T)
    NTT = T // TT
    TH = T // 2
    NTCH = TH // 128

    nc = bass.Bass("TRN2", target_bir_lowering=False)

    x_h = nc.dram_tensor("x", [T, D], F32, kind="ExternalInput")
    wq_h = nc.dram_tensor("wq", [D, D], F32, kind="ExternalInput")
    wk_h = nc.dram_tensor("wk", [D, D], F32, kind="ExternalInput")
    wv_h = nc.dram_tensor("wv", [D, D], F32, kind="ExternalInput")
    wu_h = nc.dram_tensor("wu", [D, D], F32, kind="ExternalInput")
    wo_h = nc.dram_tensor("wo", [D, D], F32, kind="ExternalInput")
    p1_h = nc.dram_tensor("p1", [HID, D], F32, kind="ExternalInput")
    b1_h = nc.dram_tensor("b1", [HID], F32, kind="ExternalInput")
    p2_h = nc.dram_tensor("p2", [D, HID], F32, kind="ExternalInput")
    b2_h = nc.dram_tensor("b2", [D], F32, kind="ExternalInput")
    out_h = nc.dram_tensor("out", [T, D], F32, kind="ExternalOutput")

    x_bf = nc.dram_tensor("x_bf", [T, D], BF16).ap()
    nu_bf = nc.dram_tensor("nu_bf", [T, D], BF16).ap()
    x2_bf = nc.dram_tensor("x2_bf", [T, D], BF16).ap()
    x2f_dr = nc.dram_tensor("x2f_dr", [T, D], F32).ap()
    p1_bf = nc.dram_tensor("p1_bf", [HID, D], BF16).ap()
    p2_bf = nc.dram_tensor("p2_bf", [D, HID], BF16).ap()
    u_dr = nc.dram_tensor("u_dr", [T, D], F32).ap()
    kv_in = nc.dram_tensor("kv_in", [128, D], F32).ap()
    kv_out = nc.dram_tensor("kv_out", [128, D], F32).ap()

    x = x_h.ap()
    out = out_h.ap()
    w_ap = {"q": wq_h.ap(), "k": wk_h.ap(), "v": wv_h.ap(),
            "u": wu_h.ap(), "o": wo_h.ap()}

    R1 = 10240
    R2 = R1 + 65536
    R3 = R2 + 65536
    R4 = R3 + 32768
    R5 = R4 + 8192
    R6 = R5 + 8192
    ARENA = R6 + 16384
    nc.alloc_sbuf_tensor("arena", [128, ARENA], I8)
    base = nc.lookup_mloc("arena").addr

    def sbuf(name, shape, dtype, off):
        return nc.alloc_sbuf_tensor_at(name, shape, dtype, offset=base + off)

    off = 0
    kvb = sbuf("kvb", [128, D], BF16, off); off += 4096
    b2b = sbuf("b2b", [1, D], BF16, off); off += 4096
    b1s = sbuf("b1s", [128, JC], F32, off); off += 256
    ones = sbuf("ones", [1, 128], BF16, off); off += 256
    ssq = sbuf("ssq", [128, H], F32, off); off += 64
    st6 = sbuf("st6", [128, 24], F32, off); off += 128
    mv = sbuf("mv", [128, 2], F32, off); off += 64
    rt = sbuf("rt", [128, H], F32, off); off += 64
    s0 = sbuf("s0", [128, H], F32, off); off += 64
    tmpn = sbuf("tmpn", [128, H], F32, off); off += 64
    teps = sbuf("teps", [128, 1], F32, off); off += 64
    st6b = sbuf("st6b", [128, 24], F32, off); off += 128
    mvb = sbuf("mvb", [128, 2], F32, off); off += 64
    rtb = sbuf("rtb", [128, 1], F32, off); off += 64
    s0b = sbuf("s0b", [128, 1], F32, off); off += 64
    tmpb = sbuf("tmpb", [128, 1], F32, off); off += 64
    tepsb = sbuf("tepsb", [128, 1], F32, off); off += 64
    assert off <= R1

    xfmb = sbuf("xfmb", [128, KC, T], BF16, R1)
    qfm = sbuf("qfm", [128, KC, T], BF16, R1 + 32768)
    hfmb = sbuf("hfmb", [128, JC, TH], BF16, R1)
    ktm = sbuf("ktm", [128, NTC, D], BF16, R2)
    kvf = sbuf("kvf", [128, D], F32, R2)
    vtm = sbuf("vtm", [128, NTC, D], BF16, R2 + NTC * D * 2)
    x2h = sbuf("x2h", [128, NTCH, D], F32, R2)
    nuT = sbuf("nuT", [128, KC, TH], BF16, R2 + 32768)
    urld = sbuf("urld", [128, 2, D], F32, R2 + 49152)
    p1T = sbuf("p1T", [128, 2, 16, 512], BF16, R2 + 32768)
    p2Tr = sbuf("p2Tr", [128, NPR, 512], BF16, R3 + 16384)
    outt = sbuf("outt", [128, D], F32, R3 + 24576)
    wstr = sbuf("wstr", [128, 2, 16, 512], BF16, R3)
    x2fmb = sbuf("x2fmb", [128, KC, TH], BF16, R6)
    ubnc = sbuf("ubnc", [128, 4, 512], F32, R4)
    xres = sbuf("xres", [128, D], F32, R4)
    nut = sbuf("nut", [128, 2, D], BF16, R5)

    prog = {k: [] for k in ("tensor", "scalar", "vector", "sync", "gpsimd")}

    def em(eng, fn):
        prog[eng].append(fn)

    def wait(eng, sem, val):
        if val > 0:
            em(eng, lambda e, s=sem.h, v=val: e.wait_ge(s, v))

    def winc(eng, sem, n=1):
        t = sem.inc(n)
        last = prog[eng].pop()
        em(eng, lambda e, f=last, s=sem.h, m=n: f(e).then_inc(s, m))
        return t

    with ExitStack() as stack:
        def mksem(name):
            return Sem(stack.enter_context(nc.semaphore(name)))

        sTe, sSc, sVe = mksem("sTe"), mksem("sSc"), mksem("sVe")
        sCc = mksem("sCc")
        ENG_SEM = {"sc": sSc, "ve": sVe, "te": sTe}
        sXbf = mksem("sXbf")
        sCst = mksem("sCst")
        sFF = mksem("sFF")
        sKv1, sKv2 = mksem("sKv1"), mksem("sKv2")
        sW = [mksem("sW0"), mksem("sW1")]
        sUb = [mksem(f"sUb{i}") for i in range(4)]
        sUr = [mksem("sUr0"), mksem("sUr1")]
        sNu = mksem("sNu")
        sX2h = [mksem("sX2a"), mksem("sX2b")]
        sXf2 = [mksem("sXfa"), mksem("sXfb")]
        sXrl = [mksem("sXra"), mksem("sXrb")]
        sXr = mksem("sXr")
        sSt = mksem("sSt")
        sTx = mksem("sTx")
        sTn = mksem("sTn")
        sTf = mksem("sTf")
        sTp = [mksem("sTp0"), mksem("sTp1")]
        sP = [mksem(f"sP{i}") for i in range(NPR)]

        def gdma(sem, dst, src, **kw):
            t = sem.inc(16)
            em("gpsimd", lambda e, d=dst, s=src, k=kw:
               e.dma_start(out=d, in_=s, **k).then_inc(sem.h, 16))
            return t

        def ytr(sem, dst, src):
            t = sem.inc(16)
            em("sync", lambda e, d=dst, s=src:
               e.dma_start_transpose(d, s).then_inc(sem.h, 16))
            return t

        def str_(sem, dst, src):
            t = sem.inc(16)
            em("scalar", lambda e, d=dst, s=src:
               e.dma_start_transpose(d, s).then_inc(sem.h, 16))
            return t

        def sdma(sem, dst, src, **kw):
            t = sem.inc(16)
            em("scalar", lambda e, d=dst, s=src, k=kw:
               e.dma_start(out=d, in_=s, **k).then_inc(sem.h, 16))
            return t

        gdma(sXbf, x_bf, x)
        gdma(sCst, b2b[0:1, :], b2_h.ap()[None, :])
        em("vector", lambda e: e.memset(ones[:], 1.0))
        t_ones = winc("vector", sVe)

        wait("sync", sXbf, sXbf.v)
        for kc in range(KC):
            ytr(sTx, xfmb[:, kc, :], x_bf[:, kc * 128:(kc + 1) * 128])

        cast_chunks = []
        for i in range(32):
            cast_chunks.append((p1_bf[i * 256:(i + 1) * 256, :],
                                p1_h.ap()[i * 256:(i + 1) * 256, :]))
        for i in range(32):
            cast_chunks.append((p2_bf[i * 64:(i + 1) * 64, :],
                                p2_h.ap()[i * 64:(i + 1) * 64, :]))

        def drip(n=1):
            for _ in range(n):
                if cast_chunks:
                    d, s = cast_chunks.pop(0)
                    gdma(sFF, d, s)

        with (
            nc.psum_tensor("kvps", [128, D], F32) as kvps,
            nc.psum_tensor("pj0", [128, 512], F32) as pj0,
            nc.psum_tensor("pj1", [128, 512], F32) as pj1,
        ):
            pj = [pj0, pj1]

            state = {"oscnt": 0}
            slot_release = [None, None]
            load_done = {}

            def load_wtiles(proj, wap, os, slot):
                if slot_release[slot] is not None:
                    wait("gpsimd", sTe, slot_release[slot])
                for k in range(16):
                    gdma(sW[slot], wstr[:, slot, k, :],
                         wap[k * 128:(k + 1) * 128, os * 512:(os + 1) * 512])
                    if proj in ("k", "v") and k in (7, 15):
                        drip()
                    elif proj in ("q", "u") and k in (5, 11, 15):
                        drip()
                load_done[(proj, os)] = (slot, sW[slot].v)

            def wait_wtiles(proj, os):
                slot, tot = load_done[(proj, os)]
                wait("tensor", sW[slot], tot)
                return slot

            pjcnt = [0]
            pj_release = {}

            def pj_guard(idx):
                if idx >= 2 and pj_release.get(idx - 2) is not None:
                    kind, tgt = pj_release[idx - 2]
                    wait("tensor", ENG_SEM[kind], tgt)

            def stream_proj_tm(proj, wap, evac):
                for os in range(4):
                    slot = state["oscnt"] % 2
                    load_wtiles(proj, wap, os, slot)
                    wait_wtiles(proj, os)
                    for tc in range(NTC):
                        idx = pjcnt[0]
                        ps = pj[idx % 2]
                        pj_guard(idx)
                        for k in range(16):
                            em("tensor", lambda e, p=ps,
                               l=xfmb[:, k, tc * 128:(tc + 1) * 128],
                               r=wstr[:, slot, k, :], st=(k == 0), sp=(k == 15):
                               e.matmul(p[:], l, r, start=st, stop=sp))
                        t = winc("tensor", sTe)
                        if tc == NTC - 1:
                            slot_release[slot] = t
                        pj_release[idx] = evac(os, tc, ps, t)
                        pjcnt[0] += 1
                    state["oscnt"] += 1

            def evac_k(os, tc, ps, t):
                wait("scalar", sTe, t)
                em("scalar", lambda e, o=ktm[:, tc, os * 512:(os + 1) * 512], p=ps:
                   e.activation(o, p[:], AF.Relu, scale=1.0 / SCALE))
                return ("sc", winc("scalar", sSc))

            def evac_v(os, tc, ps, t):
                wait("scalar", sTe, t)
                em("scalar", lambda e, o=vtm[:, tc, os * 512:(os + 1) * 512], p=ps:
                   e.copy(o, p[:]))
                return ("sc", winc("scalar", sSc))

            ucnt = [0]

            def evac_u(os, tc, ps, t):
                ui = ucnt[0] % 4
                wait("scalar", sTe, t)
                if ucnt[0] >= 4:
                    wait("scalar", sUb[ui], sUb[ui].v)
                em("scalar", lambda e, o=ubnc[:, ui, :], p=ps:
                   e.activation(o, p[:], AF.Silu))
                tsc = winc("scalar", sSc)
                wait("scalar", sSc, tsc)
                sdma(sUb[ui], u_dr[tc * 128:(tc + 1) * 128, os * 512:(os + 1) * 512],
                     ubnc[:, ui, :])
                ucnt[0] += 1
                return ("sc", tsc)

            wait("tensor", sTx, sTx.v)

            stream_proj_tm("k", w_ap["k"], evac_k)
            stream_proj_tm("v", w_ap["v"], evac_v)

            wait("tensor", sSc, sSc.v)
            for h in range(H):
                for tcc in range(NTC):
                    em("tensor", lambda e, o=kvps[:, h * 128:(h + 1) * 128],
                       l=ktm[:, tcc, h * 128:(h + 1) * 128],
                       r=vtm[:, tcc, h * 128:(h + 1) * 128],
                       st=(tcc == 0), sp=(tcc == NTC - 1):
                       e.matmul(o, l, r, start=st, stop=sp))
            t_kvmm = winc("tensor", sTe)
            wait("scalar", sTe, t_kvmm)
            em("scalar", lambda e: e.copy(kvf[:], kvps[:]))
            t_kvf = winc("scalar", sSc)
            wait("gpsimd", sSc, t_kvf)
            gdma(sKv1, kv_in, kvf[:])
            wait("gpsimd", sKv1, sKv1.v)
            em("gpsimd", lambda e: e.collective_compute(
                "AllReduce", OP.add,
                replica_groups=[[0, 1, 2, 3], [4, 5, 6, 7]],
                ins=[kv_in], outs=[kv_out]))
            t_cc = winc("gpsimd", sCc)

            stream_proj_tm("u", w_ap["u"], evac_u)

            for os in range(4):
                slot = state["oscnt"] % 2
                load_wtiles("q", w_ap["q"], os, slot)
                wait_wtiles("q", os)
                for oc in range(4):
                    for tt in range(NTT):
                        idx = pjcnt[0]
                        ps = pj[idx % 2]
                        pj_guard(idx)
                        for k in range(16):
                            em("tensor", lambda e, p=ps,
                               l=wstr[:, slot, k, oc * 128:(oc + 1) * 128],
                               r=xfmb[:, k, tt * TT:(tt + 1) * TT],
                               st=(k == 0), sp=(k == 15):
                               e.matmul(p[:, :TT], l, r, start=st, stop=sp))
                        t = winc("tensor", sTe)
                        if oc == 3 and tt == NTT - 1:
                            slot_release[slot] = t
                        ocg = os * 4 + oc
                        wait("scalar", sTe, t)
                        em("scalar", lambda e,
                           o=qfm[:, ocg, tt * TT:(tt + 1) * TT], p=ps:
                           e.activation(o, p[:, :TT], AF.Relu, scale=1.0 / SCALE))
                        pj_release[idx] = ("sc", winc("scalar", sSc))
                        pjcnt[0] += 1
                state["oscnt"] += 1
            wait("gpsimd", sCc, t_cc)
            gdma(sKv2, kvb[:], kv_out)
            for j in range(JC):
                gdma(sCst, b1s[:, j:j + 1], b1_h.ap()[j * 128:(j + 1) * 128, None])
            for back in (2, 1):
                r = pj_release.get(pjcnt[0] - back)
                if r is not None:
                    kind, tgt = r
                    wait("tensor", ENG_SEM[kind], tgt)

        t_q_evacs = sSc.v

        with (
            nc.psum_tensor("aops", [128, D], F32) as aops,
            nc.psum_tensor("yps0", [128, 512], F32) as yps0,
            nc.psum_tensor("yps1", [128, 512], F32) as yps1,
        ):
            yps = [yps0, yps1]
            wait("tensor", sVe, sVe.v)
            wait("tensor", sSc, sSc.v)
            wait("gpsimd", sTe, sTe.v)
            for i in range(4):
                wait("gpsimd", sUb[i], sUb[i].v)
            wait("tensor", sKv2, sKv2.v)
            wait("tensor", sSc, t_q_evacs)
            aops_release = None
            urld_release = [None, None]
            urld_loaded = {}
            x2f_ready = [None, None]
            x2fmb_release = [None]
            prev_tnu = 0
            ycnt = 0
            y_release = {}
            xres_release = None
            wo_last_mm = 0
            for hf in range(2):
                tc0 = hf * NTCH
                for tc4 in range(NTCH):
                    tc = tc0 + tc4
                    sl = tc % 2
                    if tc in urld_loaded:
                        t_u = urld_loaded.pop(tc)
                    else:
                        if urld_release[sl] is not None:
                            wait("gpsimd", sVe, urld_release[sl])
                        gdma(sUr[sl], urld[:, sl, :], u_dr[tc * 128:(tc + 1) * 128, :])
                        t_u = sUr[sl].v
                    drip()
                    if aops_release is not None:
                        wait("tensor", sVe, aops_release)
                    for h in range(H):
                        em("tensor", lambda e, o=aops[:, h * 128:(h + 1) * 128],
                           l=qfm[:, h, tc * 128:(tc + 1) * 128],
                           r=kvb[:, h * 128:(h + 1) * 128]:
                           e.matmul(o, l, r, start=True, stop=True))
                    t_ao = winc("tensor", sTe)
                    wait("scalar", sTe, t_ao)
                    if prev_tnu:
                        wait("scalar", sVe, prev_tnu)
                    if tc >= 2:
                        wait("scalar", sNu, sNu.v)
                    for h in range(H):
                        em("scalar", lambda e, o=nut[:, sl, h * 128:(h + 1) * 128],
                           p=aops[:, h * 128:(h + 1) * 128], a=ssq[:, h:h + 1]:
                           e.activation(o, p, AF.Square, accum_out=a))
                    t_sq = winc("scalar", sSc)
                    wait("scalar", sSc, t_sq)
                    em("scalar", lambda e: e.activation(rt[:], ssq[:], AF.Sqrt,
                                                        scale=1.0 / DH))
                    t_rt = winc("scalar", sSc)
                    wait("vector", sSc, t_rt)
                    em("vector", lambda e: e.tensor_scalar_max(rt[:], rt[:], SRMS_EPS))
                    t1 = winc("vector", sVe)
                    wait("vector", sVe, t1)
                    em("vector", lambda e: e.reciprocal(s0[:], rt[:]))
                    t2 = winc("vector", sVe)
                    wait("vector", sVe, t2)
                    em("vector", lambda e: e.tensor_mul(tmpn[:], s0[:], s0[:]))
                    t3 = winc("vector", sVe)
                    wait("vector", sVe, t3)
                    em("vector", lambda e: e.tensor_mul(tmpn[:], tmpn[:], ssq[:]))
                    t4 = winc("vector", sVe)
                    wait("vector", sVe, t4)
                    em("vector", lambda e: e.tensor_scalar(tmpn[:], tmpn[:],
                                                           -0.5 / DH, 1.5,
                                                           OP.mult, OP.add))
                    t5 = winc("vector", sVe)
                    wait("vector", sVe, t5)
                    em("vector", lambda e: e.tensor_mul(s0[:], s0[:], tmpn[:]))
                    t6 = winc("vector", sVe)
                    wait("vector", sVe, t6)
                    wait("vector", sUr[sl], t_u)
                    for h in range(H):
                        em("vector", lambda e, o=nut[:, sl, h * 128:(h + 1) * 128],
                           p=aops[:, h * 128:(h + 1) * 128], s=s0[:, h:h + 1],
                           u=urld[:, sl, h * 128:(h + 1) * 128]:
                           e.scalar_tensor_tensor(o, p, s, u, OP.mult, OP.mult))
                    t_nu = winc("vector", sVe)
                    prev_tnu = t_nu
                    aops_release = t_nu
                    urld_release[sl] = t_nu
                    wait("gpsimd", sVe, t_nu)
                    gdma(sNu, nu_bf[tc * 128:(tc + 1) * 128, :], nut[:, sl, :])
                wait("sync", sNu, sNu.v)
                if hf == 0:
                    wait("sync", sTe, sTe.v)
                else:
                    wait("sync", sTe, wo_last_mm)
                for kc in range(KC):
                    ytr(sTn, nuT[:, kc, :],
                        nu_bf[hf * TH:(hf + 1) * TH, kc * 128:(kc + 1) * 128])
                t_nuT = sTn.v
                wait("tensor", sTn, t_nuT)
                for os in range(4):
                    slot = state["oscnt"] % 2
                    load_wtiles(("o", hf), w_ap["o"], os, slot)
                    wait_wtiles(("o", hf), os)
                    for tc4 in range(NTCH):
                        tc = tc0 + tc4
                        ps = yps[ycnt % 2]
                        if ycnt >= 2 and y_release.get(ycnt - 2) is not None:
                            wait("tensor", sSc, y_release[ycnt - 2])
                        for k in range(16):
                            em("tensor", lambda e, p=ps,
                               l=nuT[:, k, tc4 * 128:(tc4 + 1) * 128],
                               r=wstr[:, slot, k, :], st=(k == 0), sp=(k == 15):
                               e.matmul(p[:], l, r, start=st, stop=sp))
                        t = winc("tensor", sTe)
                        wo_last_mm = t
                        if tc4 == NTCH - 1:
                            slot_release[slot] = t
                        wait("scalar", sTe, t)
                        if hf == 1 and os == 0 and tc4 == 0:
                            wait("scalar", sX2h[0], sX2h[0].v)
                            wait("scalar", sXf2[0], sXf2[0].v)
                        em("scalar", lambda e,
                           o=x2h[:, tc4, os * 512:(os + 1) * 512],
                           p=ps: e.copy(o, p[:]))
                        y_release[ycnt] = winc("scalar", sSc)
                        ycnt += 1
                    state["oscnt"] += 1
                t_evac_half = sSc.v
                if hf == 0:
                    for ptc in range(NTCH, min(NTCH + 2, NTC)):
                        psl = ptc % 2
                        if urld_release[psl] is not None:
                            wait("gpsimd", sVe, urld_release[psl])
                        gdma(sUr[psl], urld[:, psl, :],
                             u_dr[ptc * 128:(ptc + 1) * 128, :])
                        urld_loaded[ptc] = sUr[psl].v
                        urld_release[psl] = None
                wait("vector", sSc, t_evac_half)
                for tc4 in range(NTCH):
                    tc = tc0 + tc4
                    if xres_release is not None:
                        wait("sync", sVe, xres_release)
                    t_xr = sXr.inc(16)
                    em("sync", lambda e, d=xres[:], s=x[tc * 128:(tc + 1) * 128, :]:
                       e.dma_start(out=d, in_=s).then_inc(sXr.h, 16))
                    drip()
                    wait("vector", sXr, sXr.v)
                    em("vector", lambda e, o=x2h[:, tc4, :]:
                       e.tensor_add(o, o, xres[:]))
                    t_add = winc("vector", sVe)
                    xres_release = t_add
                    wait("vector", sVe, t_add)
                    for i in range(4):
                        em("vector", lambda e, o=st6b[:, 6 * i:6 * (i + 1)],
                           s=x2h[:, tc4, 512 * i:512 * (i + 1)]:
                           e.bn_stats(o, s))
                    t_st = winc("vector", sVe)
                    wait("vector", sVe, t_st)
                    em("vector", lambda e: e.bn_aggr(mvb[:], st6b[:]))
                    t_mv = winc("vector", sVe)
                    wait("vector", sVe, t_mv)
                    em("vector", lambda e: e.tensor_scalar_add(tepsb[:], mvb[:, 1:2],
                                                               LN_EPS))
                    t2 = winc("vector", sVe)
                    wait("scalar", sVe, t2)
                    em("scalar", lambda e: e.activation(rtb[:], tepsb[:], AF.Sqrt))
                    t_rt2 = winc("scalar", sSc)
                    wait("vector", sSc, t_rt2)
                    em("vector", lambda e: e.reciprocal(s0b[:], rtb[:]))
                    t1 = winc("vector", sVe)
                    wait("vector", sVe, t1)
                    em("vector", lambda e: e.tensor_mul(tmpb[:], s0b[:],
                                                        s0b[:]))
                    t3 = winc("vector", sVe)
                    wait("vector", sVe, t3)
                    em("vector", lambda e: e.tensor_mul(tmpb[:], tmpb[:],
                                                        tepsb[:]))
                    t4 = winc("vector", sVe)
                    wait("vector", sVe, t4)
                    em("vector", lambda e: e.tensor_scalar(tmpb[:], tmpb[:],
                                                           -0.5, 1.5, OP.mult, OP.add))
                    t5 = winc("vector", sVe)
                    wait("vector", sVe, t5)
                    em("vector", lambda e: e.tensor_mul(s0b[:], s0b[:],
                                                        tmpb[:]))
                    t6 = winc("vector", sVe)
                    wait("vector", sVe, t6)
                    em("vector", lambda e, o=x2h[:, tc4, :]:
                       e.tensor_scalar(o, o, mvb[:, 0:1], s0b[:],
                                       OP.subtract, OP.mult))
                    t_ln = winc("vector", sVe)
                    wait("gpsimd", sVe, t_ln)
                    gdma(sX2h[hf], x2_bf[tc * 128:(tc + 1) * 128, :], x2h[:, tc4, :])
                    wait("scalar", sVe, t_ln)
                    sdma(sXf2[hf], x2f_dr[tc * 128:(tc + 1) * 128, :], x2h[:, tc4, :])
                    drip()
                if hf == 0:
                    wait("sync", sX2h[0], sX2h[0].v)
                    for kc in range(KC):
                        ytr(sTf, x2fmb[:, kc, :],
                            x2_bf[0:TH, kc * 128:(kc + 1) * 128])
                    x2f_ready[0] = sTf.v

        drip(len(cast_chunks))
        assert not cast_chunks

        with nc.psum_tensor("fps", [128, 8, 512], F32) as fps:
            p1T_release = [None, None]
            p2Tr_release = [None] * NPR
            bank_rel = {}
            out_stored = 0
            wait("sync", sTe, sTe.v)
            wait("sync", sFF, sFF.v)
            wait("scalar", sFF, sFF.v)
            wait("tensor", sSc, sSc.v)
            wait("tensor", sVe, sVe.v)
            ln2_done = 0
            for hf in range(2):
                tc0 = hf * NTCH
                if hf == 1:
                    wait("sync", sX2h[1], sX2h[1].v)
                    wait("sync", sTe, x2fmb_release[0])
                    for kc in range(KC):
                        ytr(sTf, x2fmb[:, kc, :],
                            x2_bf[TH:2 * TH, kc * 128:(kc + 1) * 128])
                    x2f_ready[1] = sTf.v
                h_done = {}
                wait("tensor", sTf, x2f_ready[hf])
                wait("scalar", sCst, sCst.v)
                for jg in range(16):
                    sl = jg % 2
                    if p1T_release[sl] is not None:
                        wait("sync", sTe, p1T_release[sl])
                    for e_ in range(16):
                        ytr(sTp[sl], p1T[:, sl, e_, :],
                            p1_bf[jg * 512:(jg + 1) * 512, e_ * 128:(e_ + 1) * 128])
                    wait("tensor", sTp[sl], sTp[sl].v)
                    for j4 in range(4):
                        jc = jg * 4 + j4
                        bank = jc % 8
                        if bank_rel.get(bank) is not None:
                            kind, tgt = bank_rel[bank]
                            wait("tensor", ENG_SEM[kind], tgt)
                        for e_ in range(16):
                            em("tensor", lambda e, o=fps[:, bank, :TH],
                               l=p1T[:, sl, e_, j4 * 128:(j4 + 1) * 128],
                               r=x2fmb[:, e_, :], st=(e_ == 0), sp=(e_ == 15):
                               e.matmul(o, l, r, start=st, stop=sp))
                        t = winc("tensor", sTe)
                        if j4 == 3:
                            p1T_release[sl] = t
                        if jg == 15 and j4 == 3:
                            x2fmb_release[0] = t
                        wait("scalar", sTe, t)
                        em("scalar", lambda e, o=hfmb[:, jc, :], p=fps[:, bank, :TH],
                           b=b1s[:, jc:jc + 1]:
                           e.activation(o, p, AF.Gelu, bias=b))
                        h_done[jc] = winc("scalar", sSc)
                        bank_rel[bank] = ("sc", h_done[jc])
                for s_ in (sXf2[0], sXf2[1], sX2h[0], sX2h[1]):
                    wait("gpsimd", s_, s_.v)
                if hf == 1:
                    wait("gpsimd", sVe, ln2_done)
                for tc4 in range(NTCH):
                    tc = tc0 + tc4
                    gdma(sXrl[hf], x2h[:, tc4, :], x2f_dr[tc * 128:(tc + 1) * 128, :])
                wait("tensor", sCst, sCst.v)
                wait("vector", sVe, t_ones)
                wait("vector", sXrl[hf], sXrl[hf].v)
                es_evac = {}
                for es in range(4):
                    for tc4 in range(NTCH):
                        bank = tc4
                        if bank_rel.get(bank) is not None:
                            kind, tgt = bank_rel[bank]
                            wait("tensor", ENG_SEM[kind], tgt)
                        em("tensor", lambda e, o=fps[:, bank, :],
                           r=b2b[0:1, es * 512:(es + 1) * 512]:
                           e.matmul(o, ones[0:1, :], r, start=True, stop=False))
                    for j in range(JC):
                        rsl = j % NPR
                        if es == 0:
                            wait("tensor", sSc, h_done[j])
                        if p2Tr_release[rsl] is not None:
                            wait("sync", sTe, p2Tr_release[rsl])
                        ytr(sP[rsl], p2Tr[:, rsl, :],
                            p2_bf[es * 512:(es + 1) * 512, j * 128:(j + 1) * 128])
                        wait("tensor", sP[rsl], sP[rsl].v)
                        for tc4 in range(NTCH):
                            em("tensor", lambda e, o=fps[:, tc4, :],
                               l=hfmb[:, j, tc4 * 128:(tc4 + 1) * 128],
                               r=p2Tr[:, rsl, :], sp=(j == JC - 1):
                               e.matmul(o, l, r, start=False, stop=sp))
                        p2Tr_release[rsl] = winc("tensor", sTe)
                    t_es = sTe.v
                    for tc4 in range(NTCH):
                        wait("vector", sTe, t_es)
                        em("vector", lambda e,
                           o=x2h[:, tc4, es * 512:(es + 1) * 512],
                           p=fps[:, tc4, :]: e.tensor_add(o, o, p))
                        es_evac[(es, tc4)] = winc("vector", sVe)
                        bank_rel[tc4] = ("ve", es_evac[(es, tc4)])
                for tc4 in range(NTCH):
                    tc = tc0 + tc4
                    wait("vector", sVe, es_evac[(3, tc4)])
                    for i in range(4):
                        em("vector", lambda e, o=st6b[:, 6 * i:6 * (i + 1)],
                           s=x2h[:, tc4, 512 * i:512 * (i + 1)]:
                           e.bn_stats(o, s))
                    t_st = winc("vector", sVe)
                    wait("vector", sVe, t_st)
                    em("vector", lambda e: e.bn_aggr(mvb[:], st6b[:]))
                    t_mv = winc("vector", sVe)
                    wait("vector", sVe, t_mv)
                    em("vector", lambda e: e.tensor_scalar_add(tepsb[:], mvb[:, 1:2],
                                                               LN_EPS))
                    t2 = winc("vector", sVe)
                    wait("scalar", sVe, t2)
                    em("scalar", lambda e: e.activation(rtb[:], tepsb[:], AF.Sqrt))
                    t_rt2 = winc("scalar", sSc)
                    wait("vector", sSc, t_rt2)
                    em("vector", lambda e: e.reciprocal(s0b[:], rtb[:]))
                    t1 = winc("vector", sVe)
                    wait("vector", sVe, t1)
                    em("vector", lambda e: e.tensor_mul(tmpb[:], s0b[:],
                                                        s0b[:]))
                    t3 = winc("vector", sVe)
                    wait("vector", sVe, t3)
                    em("vector", lambda e: e.tensor_mul(tmpb[:], tmpb[:],
                                                        tepsb[:]))
                    t4 = winc("vector", sVe)
                    wait("vector", sVe, t4)
                    em("vector", lambda e: e.tensor_scalar(tmpb[:], tmpb[:],
                                                           -0.5, 1.5, OP.mult, OP.add))
                    t5 = winc("vector", sVe)
                    wait("vector", sVe, t5)
                    em("vector", lambda e: e.tensor_mul(s0b[:], s0b[:],
                                                        tmpb[:]))
                    t6 = winc("vector", sVe)
                    wait("vector", sVe, t6)
                    if out_stored:
                        wait("vector", sSt, sSt.v)
                    em("vector", lambda e, s=x2h[:, tc4, :]:
                       e.tensor_scalar(outt[:], s, mvb[:, 0:1], s0b[:],
                                       OP.subtract, OP.mult))
                    t_o = winc("vector", sVe)
                    ln2_done = t_o
                    wait("scalar", sVe, t_o)
                    sdma(sSt, out[tc * 128:(tc + 1) * 128, :], outt[:])
                    out_stored += 1

        if taps:
            tapable = {"x_bf": (x_bf, [T, D], BF16), "u_dr": (u_dr, [T, D], F32),
                       "kv_out": (kv_out, [128, D], F32), "nu_bf": (nu_bf, [T, D], BF16),
                       "x2_bf": (x2_bf, [T, D], BF16), "x2f_dr": (x2f_dr, [T, D], F32),
                       "p1_bf": (p1_bf, [HID, D], BF16), "p2_bf": (p2_bf, [D, HID], BF16),
                       "sb_hfmb": (hfmb[:, :, :], [128, JC * TH], BF16),
                       "sb_x2fmb": (x2fmb[:, :, :], [128, KC * TH], BF16),
                       "sb_x2h": (x2h[:, :, :], [128, NTCH * D], F32),
                       "sb_p1T": (p1T[:, :, :, :], [128, 2 * 16 * 512], BF16)}
            sTap = mksem("sTap")
            for s in ([sXbf, sCst, sFF, sKv1, sKv2, sNu, sXr, sSt, sTx, sTn, sTf]
                      + sW + sUb + sUr + sX2h + sXf2 + sXrl + sTp + sP):
                wait("sync", s, s.v)
            for name in taps:
                ap, shp, dt = tapable[name]
                if name.startswith("sb_"):
                    shp = [128, shp[1]] if len(shp) == 2 else shp
                th = nc.dram_tensor("tap_" + name, shp, dt, kind="ExternalOutput")
                t = sTap.inc(16)
                em("sync", lambda e, d=th.ap(), s=ap: e.dma_start(out=d, in_=s).then_inc(sTap.h, 16))
            wait("sync", sTap, sTap.v)

        wait("sync", sSt, sSt.v)
        for s in ([sXbf, sCst, sFF, sKv1, sKv2, sNu, sXr, sTx, sTn, sTf]
                  + sW + sUb + sUr + sX2h + sXf2 + sXrl + sTp + sP):
            wait("sync", s, s.v)

        with nc.Block() as block:
            @block.tensor
            def _(e):
                for f in prog["tensor"]:
                    f(e)

            @block.scalar
            def _(e):
                for f in prog["scalar"]:
                    f(e)

            @block.vector
            def _(e):
                for f in prog["vector"]:
                    f(e)

            @block.sync
            def _(e):
                for f in prog["sync"]:
                    f(e)

            @block.gpsimd
            def _(e):
                for f in prog["gpsimd"]:
                    f(e)

    return nc


_NC_CACHE = {}


def kernel(L0_attn_x, L0_attn_Wq, L0_attn_Wk, L0_attn_Wv, L0_attn_Wu,
           L0_attn_Wo, L0_ffn_proj1, L0_ffn_proj1_bias, L0_ffn_proj2,
           L0_ffn_proj2_bias, L0_ffn_gate_w):
    T = 1024
    x = np.ascontiguousarray(np.asarray(L0_attn_x, dtype=np.float32))
    B, L, _ = x.shape
    per = {
        "wq": np.ascontiguousarray(np.asarray(L0_attn_Wq, dtype=np.float32)),
        "wk": np.ascontiguousarray(np.asarray(L0_attn_Wk, dtype=np.float32)),
        "wv": np.ascontiguousarray(np.asarray(L0_attn_Wv, dtype=np.float32)),
        "wu": np.ascontiguousarray(np.asarray(L0_attn_Wu, dtype=np.float32)),
        "wo": np.ascontiguousarray(np.asarray(L0_attn_Wo, dtype=np.float32)),
        "p1": np.ascontiguousarray(np.asarray(L0_ffn_proj1, dtype=np.float32)),
        "b1": np.ascontiguousarray(np.asarray(L0_ffn_proj1_bias, dtype=np.float32)),
        "p2": np.ascontiguousarray(np.asarray(L0_ffn_proj2, dtype=np.float32)),
        "b2": np.ascontiguousarray(np.asarray(L0_ffn_proj2_bias, dtype=np.float32)),
    }
    if T not in _NC_CACHE:
        _NC_CACHE[T] = build_nc(T)
    nc = _NC_CACHE[T]
    in_maps = []
    for c in range(8):
        b = c // 4
        t0 = (c % 4) * T
        m = dict(per)
        m["x"] = np.ascontiguousarray(x[b, t0:t0 + T])
        in_maps.append(m)
    res = run_bass_kernel_spmd(nc, in_maps, core_ids=list(range(8)))
    full = np.empty((B, L, D), dtype=np.float32)
    for c in range(8):
        b = c // 4
        t0 = (c % 4) * T
        full[b, t0:t0 + T] = res.results[c]["out"]
    return full
